# version 01
# baseline (speedup 1.0000x reference)
import numpy as np
import ml_dtypes
import concourse.bass as bass
import concourse.mybir as mybir
from concourse.bass_utils import run_bass_kernel_spmd

F32 = mybir.dt.float32
BF16 = mybir.dt.bfloat16
AF = mybir.ActivationFunctionType
ALU = mybir.AluOpType
BF = ml_dtypes.bfloat16

D = 2048
T = 2048
c = 32
SC = T // c
NI = D // 128
NTB = 4
TB = 512
NO = D // 128

C_DECAY = 8.0
LN_EPS = 1e-5
RMS_EPS = 1e-6
MAGIC = 12582912.0
B7 = 7 * c


class Ctr:
    def __init__(self, s):
        self.sem = s
        self.n = 0


def build_program():
    nc = bass.Bass("TRN2", target_bir_lowering=False)

    xT = nc.dram_tensor("xT", [D, T], F32, kind="ExternalInput").ap()
    wT = nc.dram_tensor("wT", [D, D], F32, kind="ExternalInput").ap()
    projcat = nc.dram_tensor("projcat", [D, 896], F32, kind="ExternalInput").ap()
    nmw1_in = nc.dram_tensor("nmw1_in", [128, 256], F32, kind="ExternalInput").ap()
    nmw2_in = nc.dram_tensor("nmw2_in", [256, 128], F32, kind="ExternalInput").ap()
    wo_in = nc.dram_tensor("wo_in", [128, D], F32, kind="ExternalInput").ap()
    smallv = nc.dram_tensor("smallv", [128, 16], F32, kind="ExternalInput").ap()
    nmb1_in = nc.dram_tensor("nmb1_in", [128, 2], F32, kind="ExternalInput").ap()
    fbias = nc.dram_tensor("fbias", [128, 16], F32, kind="ExternalInput").ap()
    cident = nc.dram_tensor("cident", [128, 128], F32, kind="ExternalInput").ap()
    cU = nc.dram_tensor("cU", [32, 64], F32, kind="ExternalInput").ap()
    maskak_in = nc.dram_tensor("maskak_in", [64, 32], BF16, kind="ExternalInput").ap()
    maskb_in = nc.dram_tensor("maskb_in", [32, 32], BF16, kind="ExternalInput").ap()
    galle_in = nc.dram_tensor("galle_in", [1, c * B7], BF16, kind="ExternalInput").ap()
    cbias_in = nc.dram_tensor("cbias_in", [128, 8], F32, kind="ExternalInput").ap()
    x1s = nc.dram_tensor("x1s", [D, T], F32, kind="Internal").ap()
    wqs = nc.dram_tensor("wqs", [D, D], BF16, kind="Internal").ap()
    wos = nc.dram_tensor("wos", [128, D], BF16, kind="Internal").ap()
    yT = nc.dram_tensor("yT", [D, T], F32, kind="ExternalOutput").ap()
    dbg_out = nc.dram_tensor("dbg", [128, 1024], BF16, kind="ExternalOutput").ap()

    ops = {k: [] for k in ("sync", "vector", "scalar", "tensor", "gpsimd")}
    stack = []

    def sb(name, shape, dt):
        m = nc.sbuf_tensor(name, shape, dt)
        t = m.__enter__()
        stack.append(m)
        return t

    def psum(name, dt=F32):
        m = nc.psum_tensor(name, [128, 512], dt)
        t = m.__enter__()
        stack.append(m)
        return t

    def mksem(name):
        m = nc.semaphore(name)
        s = m.__enter__()
        stack.append(m)
        return Ctr(s)

    XBLK = sb("XBLK", [128, 8192], F32)
    X1B = sb("X1B", [128, 8192], F32)
    XN = sb("XN", [128, 8192], BF16)
    SQT = sb("SQT", [128, 512], BF16)
    PWS = sb("PWS", [4, 512], F32)
    WQC = sb("WQC", [128, 4096], BF16)
    PRJ = sb("PRJ", [128, 14336], BF16)
    KG = sb("KG", [128, T], BF16)
    QG = sb("QG", [128, T], BF16)
    KSUG = sb("KSUG", [128, T], BF16)
    VG = sb("VG", [128, T], BF16)
    NUG = sb("NUG", [128, T], BF16)
    MFG = sb("MFG", [128, T], BF16)
    MSG = sb("MSG", [128, T], BF16)
    OUTF = sb("OUTF", [128, T], F32)
    OUTS = sb("OUTS", [128, T], F32)
    NSG = sb("NSG", [128, T], BF16)
    LAFG = OUTF
    LASG = OUTS
    LT32 = sb("LT32", [32, 512], F32)
    EXPT = sb("EXPT", [128, 768], BF16)
    DLF = sb("DLF", [128, 64], F32)
    DLS = sb("DLS", [128, 64], F32)
    SROW = sb("SROW", [1, T], BF16)
    PWT = sb("PWT", [128, 48], F32)
    GALLE = sb("GALLE", [1, c * B7], BF16)
    GDIAG = sb("GDIAG", [1, 2 * 4 * c], BF16)
    GAKF = sb("GAKF", [64, 32], BF16)
    GAKS = sb("GAKS", [64, 32], BF16)
    GBJF = sb("GBJF", [32, 32], BF16)
    GBJS = sb("GBJS", [32, 32], BF16)
    WTRF = sb("WTRF", [32, 128], BF16)
    WTRS = sb("WTRS", [32, 128], BF16)
    MTF = sb("MTF", [32, 128], BF16)
    MTS = sb("MTS", [32, 128], BF16)
    MCP = sb("MCP", [128, 64], F32)
    WORK = sb("WORK", [128, 5 * c], F32)
    SVEC = sb("SVEC", [1, c], BF16)
    SVF32 = sb("SVF32", [1, c], F32)
    GROW = sb("GROW", [1, 2 * B7], BF16)
    DCOL = sb("DCOL", [128, 2], F32)
    JUNKV = sb("JUNKV", [128, 3 * c], F32)
    PRE32 = sb("PRE32", [128, 32], F32)
    F0 = sb("F0", [128, 128], BF16)
    S0 = sb("S0", [128, 128], BF16)
    REBT = sb("REBT", [128, 128], F32)
    OTC = REBT
    IDT = sb("IDT", [128, 128], F32)
    U32 = sb("U32", [32, 64], F32)
    MASKAK = sb("MASKAK", [64, 32], BF16)
    MASKB = sb("MASKB", [32, 32], BF16)
    ONES = sb("ONES", [128, 4], F32)
    ONESB = sb("ONESB", [128, 4], BF16)
    ONESRB = sb("ONESRB", [1, 128], BF16)
    ONESRF = sb("ONESRF", [1, 128], F32)
    SMALL = sb("SMALL", [128, 16], F32)
    SMALLB = sb("SMALLB", [128, 8], BF16)
    NMB1 = sb("NMB1", [128, 2], F32)
    FBC = sb("FBC", [128, 16], F32)
    NMW1 = sb("NMW1", [128, 256], BF16)
    NMW2 = sb("NMW2", [128, 256], BF16)
    LSF = sb("LSF", [128, 4], F32)
    ROWS = sb("ROWS", [1, 1024], F32)
    ROWSB = sb("ROWSB", [1, 2048], BF16)
    SCL = sb("SCL", [128, 8], F32)
    VABS = sb("VABS", [128, 36], F32)
    CB = sb("CB", [128, 8], F32)
    DBGT = sb("DBGT", [128, 1024], BF16)

    PB = [psum(f"PB{i}") for i in range(8)]
    BCB = [PB[0], PB[1]]
    MISC = PB[2]
    PSF = PB[3]
    PSS = PB[4]
    P5 = PB[5]
    P6 = PB[6]
    AUX = PB[7]

    din = mksem("din")
    dst = mksem("dst")
    dwq = mksem("dwq")
    dx1 = mksem("dx1")
    dga = mksem("dga")
    v_c = mksem("v_c")
    t_c = mksem("t_c")
    s_c = mksem("s_c")
    v_g = mksem("v_g")
    t_bc = mksem("t_bc")
    v_w = mksem("v_w")
    t_e = mksem("t_e")
    s_d = mksem("s_d")

    CBVALS = [1e-8, float(np.log(0.5)), LN_EPS, 1e-24, RMS_EPS, 1e-10]

    def w(eng, ctr, val=None):
        v = ctr.n if val is None else val
        s = ctr.sem
        ops[eng].append(lambda e: e.wait_ge(s, v))

    def dma(eng, ctr, out, in_):
        ctr.n += 16
        s = ctr.sem
        ops[eng].append(lambda e: e.dma_start(out=out, in_=in_).then_inc(s, 16))
        return ctr.n

    def op(eng, ctr, fn):
        if ctr is not None:
            ctr.n += 1
            s = ctr.sem
            ops[eng].append(lambda e: fn(e).then_inc(s, 1))
            return ctr.n
        ops[eng].append(lambda e: fn(e))
        return None

    for ci, cv in enumerate(CBVALS):
        nc.const_aps.register(F32, cv, CB[:, ci:ci + 1])

    dma("sync", din, IDT[:, :], cident)
    dma("sync", din, U32[:, :], cU)
    dma("sync", din, MASKAK[:, :], maskak_in)
    dma("sync", din, MASKB[:, :], maskb_in)
    dma("sync", din, SMALL[:, :], smallv)
    dma("sync", din, NMB1[:, :], nmb1_in)
    dma("sync", din, FBC[:, :], fbias)
    dma("sync", din, GALLE[0:1, 0:c * B7], galle_in)
    dma("sync", din, CB[:, :], cbias_in)
    dma("sync", din, XBLK[:, 0:256], nmw1_in)
    dma("sync", din, XBLK[:, 256:384], nmw2_in[0:128, :])
    dma("sync", din, XBLK[:, 384:512], nmw2_in[128:256, :])
    dma("sync", din, XBLK[:, 512:2560], wo_in)
    n_init = din.n

    w("vector", din, n_init)
    for fn in (
        lambda e: e.memset(ONES[:, :], 1.0),
        lambda e: e.memset(ONESB[:, :], 1.0),
        lambda e: e.memset(ONESRB[:, :], 1.0),
        lambda e: e.memset(ONESRF[:, :], 1.0),
        lambda e: e.memset(WORK[:, :], 0.0),
        lambda e: e.memset(GDIAG[:, :], 0.0),
        lambda e: e.memset(GROW[:, :], 0.0),
        lambda e: e.memset(DCOL[:, :], 0.0),
        lambda e: e.memset(F0[:, :], 0.0),
        lambda e: e.memset(S0[:, :], 0.0),
        lambda e: e.memset(DBGT[:, :], 0.0),
        lambda e: e.memset(VABS[:, :], 0.0),
        lambda e: e.memset(JUNKV[:, :], 0.0),
        lambda e: e.tensor_copy(out=NMW1[:, :], in_=XBLK[:, 0:256]),
        lambda e: e.tensor_copy(out=NMW2[:, :], in_=XBLK[:, 256:512]),
        lambda e: e.tensor_copy(out=SMALLB[:, 0:1], in_=SMALL[:, 7:8]),
    ):
        op("vector", v_c, fn)
    for j in range(4):
        op("vector", v_c, lambda e, j=j: e.tensor_scalar(
            out=NSG[:, 512 * j:512 * j + 512],
            in0=XBLK[:, 512 + 512 * j:1024 + 512 * j],
            scalar1=SMALL[:, 6:7], scalar2=None, op0=ALU.mult))
    w("gpsimd", v_c)
    dma("gpsimd", dx1, wos[:, :], NSG[:, 0:2048])
    w("scalar", din, n_init)
    op("scalar", s_c, lambda e: e.activation(out=LSF[:, 2:3], in_=SMALL[:, 0:1],
                                             func=AF.Sigmoid))
    op("scalar", s_c, lambda e: e.activation(out=LSF[:, 3:4], in_=SMALL[:, 1:2],
                                             func=AF.Sigmoid))
    op("scalar", s_c, lambda e: e.activation(out=LSF[:, 0:1], in_=LSF[:, 2:3],
                                             func=AF.Ln, bias=1e-8, scale=1.0))
    op("scalar", s_c, lambda e: e.activation(out=LSF[:, 1:2], in_=LSF[:, 3:4],
                                             func=AF.Ln, bias=1e-8, scale=1.0))
    w("vector", s_c)
    op("vector", v_c, lambda e: e.tensor_scalar(out=LSF[:, 0:2], in0=LSF[:, 0:2],
                                                scalar1=C_DECAY, scalar2=None,
                                                op0=ALU.mult))
    for i in range(NI):
        off = 896 * (i % 2)
        tgt = dma("sync", din, X1B[:, off:off + 896],
                  projcat[128 * i:128 * i + 128, :])
        w("vector", din, tgt)
        op("vector", v_c, lambda e, i=i, off=off: e.tensor_copy(
            out=PRJ[:, 896 * i:896 * i + 896], in_=X1B[:, off:off + 896]))
        if i >= 1:
            w("sync", v_c)

    abs_marks = []
    for it in range(32):
        r0 = 128 * (it // 2)
        cw = 1024 * (it % 2)
        lo = 2560 + 1024 * (it % 2)
        if it >= 2:
            w("sync", s_c, abs_marks[it - 2])
        tgt = dma("sync", din, XBLK[:, lo:lo + 1024], wT[r0:r0 + 128, cw:cw + 1024])
        w("scalar", din, tgt)
        mk = op("scalar", s_c, lambda e, it=it, lo=lo: e.activation(
            out=X1B[:, 2048:3072], in_=XBLK[:, lo:lo + 1024],
            func=AF.Abs, accum_out=VABS[:, it + 1:it + 2]))
        abs_marks.append(mk)
    w("vector", s_c)
    op("vector", v_c, lambda e: e.tensor_reduce(out=VABS[:, 0:1], in_=VABS[:, 1:33],
                                                axis=mybir.AxisListType.X, op=ALU.add))
    op("vector", v_c, lambda e: e.memset(VABS[:, 33:36], 0.0))
    w("tensor", din, n_init)
    w("tensor", v_c)
    op("tensor", t_c, lambda e: e.matmul(out=MISC[0:1, 0:1], lhsT=VABS[:, 0:1],
                                         rhs=ONES[:, 0:1], start=True, stop=True,
                                         skip_group_check=True))
    w("scalar", t_c)
    op("scalar", s_c, lambda e: e.activation(out=SCL[0:1, 4:5], in_=MISC[0:1, 0:1],
                                             func=AF.Copy, scale=1.0 / (D * D),
                                             bias=1e-10))
    op("scalar", s_c, lambda e: e.activation(out=SCL[0:1, 1:2], in_=MISC[0:1, 0:1],
                                             func=AF.Copy, scale=0.5 / (D * D)))
    w("vector", s_c)
    op("vector", v_c, lambda e: e.reciprocal(out=SCL[0:1, 0:1], in_=SCL[0:1, 4:5]))
    op("vector", v_c, lambda e: e.memset(VABS[:, 4:8], 0.0))
    w("tensor", s_c)
    w("tensor", v_c)
    op("tensor", t_c, lambda e: e.matmul(out=MISC[0:128, 8:10], lhsT=ONESRF[0:1, :],
                                         rhs=SCL[0:1, 0:2], start=True, stop=True,
                                         skip_group_check=True))
    w("vector", t_c)
    op("vector", v_c, lambda e: e.tensor_copy(out=SCL[:, 2:4], in_=MISC[:, 8:10]))
    store_marks = []
    for it in range(32):
        r0 = 128 * (it // 2)
        cw = 1024 * (it % 2)
        lo = 2560 + 1024 * (it % 2)
        xo = 1024 * (it % 2)
        if it >= 2:
            w("sync", v_c)
        tgt = dma("sync", din, XBLK[:, lo:lo + 1024], wT[r0:r0 + 128, cw:cw + 1024])
        w("vector", din, tgt)
        op("vector", v_c, lambda e, lo=lo, xo=xo: e.tensor_scalar(
            out=X1B[:, xo:xo + 1024], in0=XBLK[:, lo:lo + 1024],
            scalar1=SCL[:, 2:3], scalar2=MAGIC, op0=ALU.mult, op1=ALU.add))
        op("vector", v_c, lambda e, xo=xo: e.tensor_scalar(
            out=X1B[:, xo:xo + 1024], in0=X1B[:, xo:xo + 1024],
            scalar1=-MAGIC, scalar2=1.0, op0=ALU.add, op1=ALU.min))
        if it >= 2:
            w("vector", dx1, store_marks[it - 2])
        op("vector", v_c, lambda e, xo=xo: e.tensor_scalar(
            out=XN[:, xo:xo + 1024], in0=X1B[:, xo:xo + 1024],
            scalar1=-1.0, scalar2=None, op0=ALU.max))
        w("gpsimd", v_c)
        store_marks.append(dma("gpsimd", dx1, wqs[r0:r0 + 128, cw:cw + 1024],
                               XN[:, xo:xo + 1024]))
    wq_done = dx1.n

    def emit_ln(SRC):
        for i in range(NI):
            co = 512 * i
            op("vector", v_c, lambda e, co=co: e.tensor_copy(
                out=XN[:, co:co + 512], in_=SRC[:, co:co + 512]))
            op("vector", v_c, lambda e, co=co: e.scalar_tensor_tensor(
                out=SQT[:, 0:512], in0=XN[:, co:co + 512], scalar=1.0,
                in1=XN[:, co:co + 512], op0=ALU.mult, op1=ALU.mult))
            w("tensor", v_c)
            op("tensor", t_c, lambda e, i=i: e.matmul(
                out=MISC[0:1, 0:512], lhsT=ONESB[:, 0:1],
                rhs=XN[:, 512 * i:512 * i + 512],
                start=(i == 0), stop=(i == NI - 1), skip_group_check=True))
            op("tensor", t_c, lambda e, i=i: e.matmul(
                out=AUX[0:1, 0:512], lhsT=ONESB[:, 0:1], rhs=SQT[:, 0:512],
                start=(i == 0), stop=(i == NI - 1), skip_group_check=True))
        w("scalar", t_c)
        op("scalar", s_c, lambda e: e.activation(out=ROWS[0:1, 0:512],
                                                 in_=MISC[0:1, 0:512],
                                                 func=AF.Copy, scale=1.0 / D))
        w("vector", s_c)
        op("vector", v_c, lambda e: e.tensor_tensor(
            out=ROWS[0:1, 512:1024], in0=ROWS[0:1, 0:512],
            in1=ROWS[0:1, 0:512], op=ALU.mult))
        op("vector", v_c, lambda e: e.scalar_tensor_tensor(
            out=PWS[0:1, 0:512], in0=AUX[0:1, 0:512], scalar=1.0 / D,
            in1=ROWS[0:1, 512:1024], op0=ALU.mult, op1=ALU.subtract))
        w("scalar", v_c)
        op("scalar", s_c, lambda e: e.activation(out=ROWS[0:1, 512:1024],
                                                 in_=PWS[0:1, 0:512],
                                                 func=AF.Sqrt, bias=LN_EPS,
                                                 scale=1.0))
        op("scalar", s_c, lambda e: e.activation(out=ROWSB[0:1, 512:1024],
                                                 in_=ROWS[0:1, 0:512], func=AF.Copy))
        w("vector", s_c)
        op("vector", v_c, lambda e: e.reciprocal(out=ROWS[0:1, 0:512],
                                                 in_=ROWS[0:1, 512:1024]))
        op("vector", v_c, lambda e: e.memset(VABS[:, 4:8], 0.0))
        op("vector", v_c, lambda e: e.tensor_copy(out=ROWSB[0:1, 0:512],
                                                  in_=ROWS[0:1, 0:512]))
        w("tensor", s_c)
        w("tensor", v_c)
        op("tensor", t_c, lambda e: e.matmul(out=MISC[:, 0:512], lhsT=ONESRB[0:1, :],
                                             rhs=ROWSB[0:1, 512:1024], start=True,
                                             stop=True, skip_group_check=True))
        op("tensor", t_c, lambda e: e.matmul(out=AUX[:, 0:512], lhsT=ONESRB[0:1, :],
                                             rhs=ROWSB[0:1, 0:512], start=True,
                                             stop=True, skip_group_check=True))
        w("vector", t_c)
        for i in range(NI):
            co = 512 * i
            op("vector", v_c, lambda e, co=co: e.tensor_tensor(
                out=XN[:, co:co + 512], in0=SRC[:, co:co + 512],
                in1=MISC[:, 0:512], op=ALU.subtract))
            op("vector", v_c, lambda e, co=co: e.tensor_tensor(
                out=XN[:, co:co + 512], in0=XN[:, co:co + 512],
                in1=AUX[:, 0:512], op=ALU.mult))

    wqc_marks = []
    ffn_state = {"prev_store": 0}

    def emit_ffn(SRC, DSTF, store_to, tc0):
        w("tensor", v_c)
        w("vector", dx1, ffn_state["prev_store"])
        evac_done = []
        for o in range(NO):
            par = o % 2
            if len(wqc_marks) >= 2:
                w("sync", t_c, wqc_marks[-2])
            tgt = dma("sync", dwq,
                      WQC[:, 2048 * par:2048 * par + 2048]
                      .rearrange("p (a n) -> p a n", a=16),
                      wqs.rearrange("(a p) n -> p a n", p=128)
                      [:, :, 128 * o:128 * o + 128])
            w("tensor", dwq, tgt)
            if o >= 2:
                w("tensor", v_c, evac_done[o - 2])
            for i in range(NI):
                last = (i == NI - 1)
                mk = op("tensor", t_c if last else None,
                        lambda e, i=i, par=par, last=last: e.matmul(
                            out=PB[5 + par][:, 0:512],
                            lhsT=WQC[:, 2048 * par + 128 * i:
                                      2048 * par + 128 * i + 128],
                            rhs=XN[:, 512 * i:512 * i + 512],
                            start=(i == 0), stop=last, skip_group_check=True))
                if last:
                    wqc_marks.append(mk)
            w("vector", t_c)
            co = 512 * o
            op("vector", v_c, lambda e, co=co, par=par: e.scalar_tensor_tensor(
                out=DSTF[:, co:co + 512], in0=PB[5 + par][:, 0:512],
                scalar=SCL[:, 3:4], in1=SRC[:, co:co + 512],
                op0=ALU.mult, op1=ALU.add))
            ev = op("vector", v_c, lambda e, co=co, o=o: e.tensor_scalar(
                out=DSTF[:, co:co + 512], in0=DSTF[:, co:co + 512],
                scalar1=FBC[:, o:o + 1], scalar2=None, op0=ALU.add))
            evac_done.append(ev)
            if store_to is not None:
                w("gpsimd", v_c, ev)
                dma("gpsimd", dx1, store_to[128 * o:128 * o + 128, tc0:tc0 + 512],
                    DSTF[:, co:co + 512])
        if store_to is not None:
            ffn_state["prev_store"] = dx1.n

    w("sync", dx1, wq_done)
    xblk_free = 0
    for tb in range(NTB):
        tc0 = TB * tb
        if tb >= 1:
            w("sync", v_c, xblk_free)
        for i in range(NI):
            dma("sync", din, XBLK[:, 512 * i:512 * i + 512],
                xT[128 * i:128 * i + 128, tc0:tc0 + TB])
        w("vector", din)
        w("tensor", din)
        emit_ln(XBLK)
        emit_ffn(XBLK, X1B, x1s, tc0)
        xblk_free = v_c.n
        emit_ln(X1B)
        if tb == 0:
            op("vector", v_c, lambda e: e.tensor_copy(out=DBGT[:, 0:32],
                                                      in_=XN[:, 0:32]))

        w("tensor", v_c)
        for p in range(7):
            for i in range(NI):
                last = (i == NI - 1)
                op("tensor", t_c if last else None,
                   lambda e, p=p, i=i, last=last: e.matmul(
                       out=PB[p][:, 0:512],
                       lhsT=PRJ[:, 896 * i + 128 * p:896 * i + 128 * p + 128],
                       rhs=XN[:, 512 * i:512 * i + 512],
                       start=(i == 0), stop=last, skip_group_check=True))
        proj_done = t_c.n

        w("vector", t_c, proj_done)
        op("vector", v_c, lambda e: e.tensor_scalar(
            out=PWS[0:3, 0:512], in0=PB[6][0:3, 0:512],
            scalar1=SMALL[0:3, 9:10], scalar2=None, op0=ALU.add))
        w("tensor", v_c)
        for t4 in range(4):
            op("tensor", t_c, lambda e, t4=t4: e.transpose(
                out=AUX[:, 3 * t4:3 * t4 + 3],
                in_=PWS[0:3, 128 * t4:128 * t4 + 128], identity=IDT[0:3, 0:3]))
        w("vector", t_c)
        w("vector", s_c)
        op("vector", v_c, lambda e: e.tensor_reduce(
            out=SCL[:, 4:8], in_=AUX[:, 0:12].rearrange("p (a b) -> p a b", a=4),
            axis=mybir.AxisListType.X, op=ALU.max))
        op("vector", v_c, lambda e, tc0=tc0: e.tensor_copy(
            out=VG[:, tc0:tc0 + 512], in_=PB[1][:, 0:512]))
        op("vector", v_c, lambda e: e.tensor_scalar(
            out=SCL[:, 4:8], in0=SCL[:, 4:8], scalar1=-1.0, scalar2=None,
            op0=ALU.mult))
        op("vector", v_c, lambda e: e.memset(VABS[:, 4:8], 0.0))
        if tb == 0:
            op("vector", v_c, lambda e: e.tensor_copy(out=DBGT[:, 882:886],
                                                      in_=SCL[:, 4:8]))
        w("scalar", v_c)
        for t4 in range(4):
            op("scalar", s_c, lambda e, t4=t4: e.activation(
                out=XN[:, 1024 + 4 * t4:1024 + 4 * t4 + 3],
                in_=AUX[:, 3 * t4:3 * t4 + 3], func=AF.Exp,
                bias=SCL[:, 4 + t4:5 + t4], scale=1.0))
        w("vector", s_c)
        op("vector", v_c, lambda e: e.tensor_reduce(
            out=SCL[:, 4:8],
            in_=XN[:, 1024:1040].rearrange("p (a b) -> p a b", a=4)[:, :, 0:3],
            axis=mybir.AxisListType.X, op=ALU.add))
        op("vector", v_c, lambda e: e.memset(VABS[:, 4:8], 0.0))
        op("vector", v_c, lambda e: e.memset(VABS[:, 8:12], 0.0))
        op("vector", v_c, lambda e: e.reciprocal(out=SCL[:, 4:8], in_=SCL[:, 4:8]))
        op("vector", v_c, lambda e: e.memset(VABS[:, 8:12], 0.0))
        for t4 in range(4):
            op("vector", v_c, lambda e, t4=t4, tb=tb: e.tensor_scalar(
                out=PWT[:, 12 * tb + 3 * t4:12 * tb + 3 * t4 + 3],
                in0=XN[:, 1024 + 4 * t4:1024 + 4 * t4 + 3],
                scalar1=SCL[:, 4 + t4:5 + t4], scalar2=None, op0=ALU.mult))
        if tb == 0:
            op("vector", v_c, lambda e: e.tensor_copy(out=DBGT[:, 517:520],
                                                      in_=PWT[:, 3:6]))
            op("vector", v_c, lambda e: e.tensor_copy(out=DBGT[:, 872:876],
                                                      in_=SCL[:, 4:8]))
            op("vector", v_c, lambda e: e.tensor_copy(out=DBGT[:, 876:879],
                                                      in_=XN[:, 1028:1031]))
            op("vector", v_c, lambda e: e.tensor_copy(out=DBGT[:, 879:882],
                                                      in_=AUX[:, 3:6]))

        if tb == 0:
            op("vector", v_c, lambda e: e.tensor_copy(out=DBGT[:, 32:64],
                                                      in_=VG[:, 0:32]))
        w("tensor", v_c)
        w("tensor", s_c)
        op("tensor", t_c, lambda e, tc0=tc0: e.matmul(
            out=AUX[:, 0:512], lhsT=NMW1[:, 0:128], rhs=VG[:, tc0:tc0 + 512],
            start=True, stop=True, skip_group_check=True))
        op("tensor", t_c, lambda e, tc0=tc0: e.matmul(
            out=PB[6][:, 0:512], lhsT=NMW1[:, 128:256], rhs=VG[:, tc0:tc0 + 512],
            start=True, stop=True, skip_group_check=True))
        w("scalar", t_c)
        op("scalar", s_c, lambda e: e.activation(
            out=XN[:, 0:512], in_=AUX[:, 0:512], func=AF.Gelu,
            bias=NMB1[:, 0:1], scale=1.0))
        op("scalar", s_c, lambda e: e.activation(
            out=XN[:, 512:1024], in_=PB[6][:, 0:512], func=AF.Gelu,
            bias=NMB1[:, 1:2], scale=1.0))
        w("tensor", s_c)
        for h in range(2):
            op("tensor", t_c, lambda e, h=h: e.matmul(
                out=AUX[:, 0:512], lhsT=NMW2[:, 128 * h:128 * h + 128],
                rhs=XN[:, 512 * h:512 * h + 512], start=(h == 0), stop=(h == 1),
                skip_group_check=True))
        w("vector", t_c)
        op("vector", v_c, lambda e, tc0=tc0: e.tensor_scalar(
            out=NUG[:, tc0:tc0 + 512], in0=AUX[:, 0:512],
            scalar1=SMALL[:, 4:5], scalar2=None, op0=ALU.add))
        if tb == 0:
            op("vector", v_c, lambda e: e.tensor_copy(out=DBGT[:, 480:512],
                                                      in_=NUG[:, 0:32]))

        for (pch, DST) in ((0, KG), (2, QG)):
            op("vector", v_c, lambda e, pch=pch: e.tensor_copy(
                out=XN[:, 0:512], in_=PB[pch][:, 0:512]))
            op("vector", v_c, lambda e: e.scalar_tensor_tensor(
                out=SQT[:, 0:512], in0=XN[:, 0:512], scalar=1.0,
                in1=XN[:, 0:512], op0=ALU.mult, op1=ALU.mult))
            w("tensor", v_c)
            op("tensor", t_c, lambda e: e.matmul(
                out=AUX[0:1, 0:512], lhsT=ONESB[:, 0:1], rhs=SQT[:, 0:512],
                start=True, stop=True, skip_group_check=True))
            w("scalar", t_c)
            op("scalar", s_c, lambda e: e.activation(
                out=ROWS[0:1, 512:1024], in_=AUX[0:1, 0:512], func=AF.Sqrt,
                bias=1e-24, scale=1.0))
            w("vector", s_c)
            op("vector", v_c, lambda e: e.reciprocal(
                out=ROWS[0:1, 0:512], in_=ROWS[0:1, 512:1024]))
            op("vector", v_c, lambda e: e.memset(VABS[:, 4:8], 0.0))
            op("vector", v_c, lambda e: e.tensor_copy(
                out=ROWSB[0:1, 1024:1536], in_=ROWS[0:1, 0:512]))
            w("tensor", v_c)
            op("tensor", t_c, lambda e: e.matmul(
                out=AUX[:, 0:512], lhsT=ONESRB[0:1, :],
                rhs=ROWSB[0:1, 1024:1536], start=True, stop=True,
                skip_group_check=True))
            w("vector", t_c)
            op("vector", v_c, lambda e, tc0=tc0, DST=DST: e.tensor_tensor(
                out=DST[:, tc0:tc0 + 512], in0=XN[:, 0:512], in1=AUX[:, 0:512],
                op=ALU.mult))

        w("scalar", t_c)
        op("scalar", s_c, lambda e: e.activation(
            out=XN[:, 0:512], in_=PB[5][:, 0:512], func=AF.Gelu,
            bias=SMALL[:, 5:6], scale=1.0))
        w("tensor", s_c)
        op("tensor", t_c, lambda e: e.matmul(
            out=AUX[0:1, 0:512], lhsT=SMALLB[:, 0:1], rhs=XN[:, 0:512],
            start=True, stop=True, skip_group_check=True))
        w("scalar", t_c)
        op("scalar", s_c, lambda e: e.activation(
            out=ROWS[0:1, 0:512], in_=AUX[0:1, 0:512], func=AF.Sigmoid,
            bias=SMALL[0:1, 8:9], scale=1.0))
        w("vector", s_c)
        op("vector", v_c, lambda e: e.tensor_scalar(
            out=ROWSB[0:1, 1024:1536], in0=ROWS[0:1, 0:512], scalar1=0.5,
            scalar2=None, op0=ALU.is_gt))
        w("tensor", v_c)
        op("tensor", t_c, lambda e: e.matmul(
            out=AUX[:, 0:512], lhsT=ONESRB[0:1, :], rhs=ROWSB[0:1, 1024:1536],
            start=True, stop=True, skip_group_check=True))
        w("vector", t_c)
        op("vector", v_c, lambda e, tc0=tc0: e.tensor_tensor(
            out=KSUG[:, tc0:tc0 + 512], in0=KG[:, tc0:tc0 + 512],
            in1=AUX[:, 0:512], op=ALU.mult))
        if tb == 0:
            op("vector", v_c, lambda e: e.tensor_copy(out=DBGT[:, 64:96],
                                                      in_=KG[:, 0:32]))
            op("vector", v_c, lambda e: e.tensor_copy(out=DBGT[:, 96:128],
                                                      in_=QG[:, 0:32]))
            op("vector", v_c, lambda e: e.tensor_copy(out=DBGT[:, 128:160],
                                                      in_=KSUG[:, 0:32]))

        w("scalar", t_c)
        op("scalar", s_c, lambda e: e.activation(
            out=XN[:, 0:512], in_=PB[3][:, 0:512], func=AF.Sigmoid,
            bias=SMALL[:, 2:3], scale=1.0))
        op("scalar", s_c, lambda e: e.activation(
            out=XN[:, 512:1024], in_=PB[4][:, 0:512], func=AF.Sigmoid,
            bias=SMALL[:, 3:4], scale=1.0))
        w("vector", s_c)
        op("vector", v_c, lambda e, tc0=tc0: e.tensor_scalar(
            out=LAFG[:, tc0:tc0 + 512], in0=XN[:, 0:512],
            scalar1=LSF[:, 0:1], scalar2=None, op0=ALU.mult))
        op("vector", v_c, lambda e, tc0=tc0: e.tensor_scalar(
            out=LASG[:, tc0:tc0 + 512], in0=XN[:, 512:1024],
            scalar1=LSF[:, 1:2], scalar2=None, op0=ALU.mult))
        if tb == 0:
            op("vector", v_c, lambda e: e.tensor_copy(out=DBGT[:, 160:192],
                                                      in_=LAFG[:, 0:32]))
            op("vector", v_c, lambda e: e.tensor_copy(out=DBGT[:, 192:224],
                                                      in_=LASG[:, 0:32]))

    expt_free = 0
    for tt in range(16):
        t0 = 128 * tt
        for (st, LG) in ((0, LAFG), (1, LASG)):
            w("tensor", v_c)
            for sub in range(4):
                op("tensor", t_c, lambda e, sub=sub, t0=t0, LG=LG: e.transpose(
                    out=MISC[0:32, 128 * sub:128 * sub + 128],
                    in_=LG[:, t0 + 32 * sub:t0 + 32 * sub + 32],
                    identity=IDT[:, :]))
            w("vector", t_c)
            for sub in range(4):
                op("vector", v_c, lambda e, sub=sub: e.tensor_copy(
                    out=LT32[0:32, 128 * sub:128 * sub + 128],
                    in_=MISC[0:32, 128 * sub:128 * sub + 128]))
            w("tensor", v_c)
            for sub in range(4):
                for ut in range(2):
                    op("tensor", t_c, lambda e, st=st, sub=sub, ut=ut: e.matmul(
                        out=AUX[:, 256 * st + 128 * ut + 32 * sub:
                                256 * st + 128 * ut + 32 * sub + 32],
                        lhsT=LT32[0:32, 128 * sub:128 * sub + 128],
                        rhs=U32[:, 32 * ut:32 * ut + 32],
                        start=True, stop=True, skip_group_check=True))
        w("scalar", t_c)
        for st in range(2):
            b = 0
            sp = 256 * st
            w("scalar", v_c, expt_free)
            LG = LAFG if st == 0 else LASG
            for fn in (
                lambda e, b=b, sp=sp: e.activation(
                    out=EXPT[:, b:b + 128], in_=AUX[:, sp:sp + 128], func=AF.Exp,
                    bias=float(np.log(0.5)), scale=1.0),
                lambda e, b=b, sp=sp: e.activation(
                    out=EXPT[:, b + 128:b + 256], in_=AUX[:, sp:sp + 128],
                    func=AF.Exp),
                lambda e, b=b, sp=sp: e.activation(
                    out=EXPT[:, b + 384:b + 512], in_=AUX[:, sp + 128:sp + 256],
                    func=AF.Exp, scale=-1.0),
                lambda e, b=b, sp=sp: e.activation(
                    out=EXPT[:, b + 512:b + 640], in_=AUX[:, sp + 128:sp + 256],
                    func=AF.Exp),
                lambda e, t0=t0, LG=LG: e.activation(
                    out=X1B[:, 1024:1152], in_=LG[:, t0:t0 + 128], func=AF.Exp),
            ):
                op("scalar", s_c, fn)
            w("vector", s_c)
            QA3 = 6144 * st
            KSRC = KG if st == 0 else KSUG
            MG = MFG if st == 0 else MSG
            DL = DLF if st == 0 else DLS
            blk0 = 96 * 4 * tt
            op("vector", v_c, lambda e: e.tensor_scalar(
                out=SQT[:, 0:128], in0=X1B[:, 1024:1152],
                scalar1=-1.0, scalar2=1.0, op0=ALU.mult, op1=ALU.add))
            op("vector", v_c, lambda e, b=b, t0=t0, QA3=QA3, blk0=blk0: e.tensor_tensor(
                out=PRJ[:, QA3 + blk0:QA3 + blk0 + 384]
                .rearrange("p (a b) -> p a b", b=96)[:, :, 0:32],
                in0=QG[:, t0:t0 + 128].rearrange("p (a b) -> p a b", b=32),
                in1=EXPT[:, b:b + 128].rearrange("p (a b) -> p a b", b=32),
                op=ALU.mult))
            op("vector", v_c, lambda e, b=b, t0=t0, QA3=QA3, blk0=blk0: e.tensor_tensor(
                out=PRJ[:, QA3 + blk0:QA3 + blk0 + 384]
                .rearrange("p (a b) -> p a b", b=96)[:, :, 32:64],
                in0=KG[:, t0:t0 + 128].rearrange("p (a b) -> p a b", b=32),
                in1=EXPT[:, b + 128:b + 256].rearrange("p (a b) -> p a b", b=32),
                op=ALU.mult))
            op("vector", v_c, lambda e, b=b, t0=t0, QA3=QA3, blk0=blk0: e.tensor_tensor(
                out=PRJ[:, QA3 + blk0:QA3 + blk0 + 384]
                .rearrange("p (a b) -> p a b", b=96)[:, :, 64:96],
                in0=QG[:, t0:t0 + 128].rearrange("p (a b) -> p a b", b=32),
                in1=EXPT[:, b + 512:b + 640].rearrange("p (a b) -> p a b", b=32),
                op=ALU.mult))
            op("vector", v_c, lambda e, t0=t0, KSRC=KSRC: e.scalar_tensor_tensor(
                out=SQT[:, 128:256], in0=KSRC[:, t0:t0 + 128], scalar=1.0,
                in1=SQT[:, 0:128], op0=ALU.mult, op1=ALU.mult))
            op("vector", v_c, lambda e, b=b, t0=t0, MG=MG: e.scalar_tensor_tensor(
                out=MG[:, t0:t0 + 128], in0=SQT[:, 128:256], scalar=1.0,
                in1=EXPT[:, b + 384:b + 512], op0=ALU.mult, op1=ALU.mult))
            op("vector", v_c, lambda e, b=b, tt=tt, DL=DL: e.tensor_copy(
                out=DL[:, 4 * tt:4 * tt + 4],
                in_=EXPT[:, b + 512 + 31:b + 640:32]))
            expt_free = v_c.n
        if tt == 0:
            op("vector", v_c, lambda e: e.tensor_copy(out=DBGT[:, 224:256],
                                                      in_=MFG[:, 0:32]))
            op("vector", v_c, lambda e: e.tensor_copy(out=DBGT[:, 256:288],
                                                      in_=MSG[:, 0:32]))
            op("vector", v_c, lambda e: e.tensor_copy(out=DBGT[:, 512:608],
                                                      in_=PRJ[:, 0:96]))
            op("vector", v_c, lambda e: e.tensor_copy(out=DBGT[:, 608:704],
                                                      in_=PRJ[:, 6144:6240]))
            op("vector", v_c, lambda e: e.tensor_copy(out=DBGT[:, 832:836],
                                                      in_=DLF[:, 0:4]))
            op("vector", v_c, lambda e: e.tensor_copy(out=DBGT[:, 836:840],
                                                      in_=DLS[:, 0:4]))

    for k in range(SC):
        par = k % 2
        gb = 0
        gdb = par * 4 * c
        t0 = c * k
        pp = par * 256

        w("tensor", v_c)
        op("tensor", t_c, lambda e, k=k: e.matmul(
            out=MISC[0:64, 0:32], lhsT=PRJ[:, 96 * k:96 * k + 64],
            rhs=MFG[:, c * k:c * k + 32], start=True, stop=True,
            skip_group_check=True))
        op("tensor", t_c, lambda e, k=k: e.matmul(
            out=MISC[0:64, 32:64], lhsT=PRJ[:, 6144 + 96 * k:6144 + 96 * k + 64],
            rhs=MSG[:, c * k:c * k + 32], start=True, stop=True,
            skip_group_check=True))
        op("tensor", t_c, lambda e, k=k: e.matmul(
            out=MISC[0:32, 64:96], lhsT=MFG[:, c * k:c * k + 32],
            rhs=PRJ[:, 96 * k + 64:96 * k + 96], start=True, stop=True,
            skip_group_check=True))
        op("tensor", t_c, lambda e, k=k: e.matmul(
            out=MISC[0:32, 96:128], lhsT=MSG[:, c * k:c * k + 32],
            rhs=PRJ[:, 6144 + 96 * k + 64:6144 + 96 * k + 96], start=True,
            stop=True, skip_group_check=True))
        for sl, (QA3, MG) in enumerate(((0, MFG), (6144, MSG))):
            if sl == 1:
                w("vector", t_c)
            op("vector", v_c, lambda e, k=k, QA3=QA3, MG=MG: e.scalar_tensor_tensor(
                out=SQT[:, 0:31], in0=PRJ[:, QA3 + 96 * k + 1:QA3 + 96 * k + 32],
                scalar=1.0, in1=MG[:, c * k:c * k + 31],
                op0=ALU.mult, op1=ALU.mult))
            op("vector", v_c, lambda e, k=k, QA3=QA3, MG=MG: e.scalar_tensor_tensor(
                out=SQT[:, 32:63], in0=PRJ[:, QA3 + 96 * k + 33:QA3 + 96 * k + 64],
                scalar=1.0, in1=MG[:, c * k:c * k + 31],
                op0=ALU.mult, op1=ALU.mult))
            w("tensor", v_c)
            op("tensor", t_c, lambda e, sl=sl: e.matmul(
                out=MISC[0:1, 128 + 64 * sl:128 + 64 * sl + 31],
                lhsT=ONESB[:, 0:1], rhs=SQT[:, 0:31], start=True, stop=True,
                skip_group_check=True))
            op("tensor", t_c, lambda e, sl=sl: e.matmul(
                out=MISC[0:1, 160 + 64 * sl:160 + 64 * sl + 31],
                lhsT=ONESB[:, 0:1], rhs=SQT[:, 32:63], start=True, stop=True,
                skip_group_check=True))
        w("vector", t_c)
        op("vector", v_c, lambda e: e.tensor_tensor(
            out=GAKF[:, :], in0=MISC[0:64, 0:32], in1=MASKAK[:, :], op=ALU.mult))
        op("vector", v_c, lambda e: e.tensor_tensor(
            out=GAKS[:, :], in0=MISC[0:64, 32:64], in1=MASKAK[:, :], op=ALU.mult))
        op("vector", v_c, lambda e: e.tensor_tensor(
            out=GBJF[:, :], in0=MISC[0:32, 64:96], in1=MASKB[:, :], op=ALU.mult))
        op("vector", v_c, lambda e: e.tensor_tensor(
            out=GBJS[:, :], in0=MISC[0:32, 96:128], in1=MASKB[:, :], op=ALU.mult))
        op("vector", v_c, lambda e, gdb=gdb: e.tensor_scalar(
            out=GDIAG[0:1, gdb + 4:gdb + 128:4], in0=MISC[0:1, 128:159],
            scalar1=-1.0, scalar2=None, op0=ALU.mult))
        op("vector", v_c, lambda e, gdb=gdb: e.tensor_scalar(
            out=GDIAG[0:1, gdb + 5:gdb + 128:4], in0=MISC[0:1, 192:223],
            scalar1=-1.0, scalar2=None, op0=ALU.mult))
        op("vector", v_c, lambda e, gdb=gdb: e.tensor_scalar(
            out=GDIAG[0:1, gdb + 6:gdb + 128:4], in0=MISC[0:1, 160:191],
            scalar1=-1.0, scalar2=None, op0=ALU.mult))
        vdiag = op("vector", v_c, lambda e, gdb=gdb: e.tensor_scalar(
            out=GDIAG[0:1, gdb + 7:gdb + 128:4], in0=MISC[0:1, 224:255],
            scalar1=-1.0, scalar2=None, op0=ALU.mult))
        if k == 0:
            op("vector", v_c, lambda e: e.tensor_copy(out=DBGT[0:64, 704:736],
                                                      in_=GAKF[:, :]))
            op("vector", v_c, lambda e: e.tensor_copy(out=DBGT[0:64, 736:768],
                                                      in_=GAKS[:, :]))
            op("vector", v_c, lambda e: e.tensor_copy(out=DBGT[0:32, 768:800],
                                                      in_=GBJF[:, :]))
            op("vector", v_c, lambda e: e.tensor_copy(out=DBGT[0:32, 800:832],
                                                      in_=GBJS[:, :]))
            op("vector", v_c, lambda e: e.tensor_copy(out=DBGT[0:1, 896:1024],
                                                      in_=GDIAG[0:1, 0:128]))
        w("gpsimd", v_c, vdiag)
        for (src, blk) in ((GAKF[0:32, :], 0), (GAKS[0:32, :], 2),
                           (GAKF[32:64, :], 4), (GAKS[32:64, :], 5)):
            dma("gpsimd", dga,
                GALLE[0:1, gb:gb + c * B7]
                .rearrange("p (a b) -> p a b", b=B7)[:, :, blk * c:blk * c + c],
                src)
        ga_done = dga.n

        if k == 4:
            op("vector", v_c, lambda e: e.tensor_copy(out=DBGT[:, 840:872],
                                                      in_=F0[:, 0:32]))
        w("tensor", v_c)
        op("tensor", t_c, lambda e, k=k, pp=pp: e.matmul(
            out=PSF[:, pp:pp + 96], lhsT=F0[:, :],
            rhs=PRJ[:, 96 * k:96 * k + 96], start=True, stop=True,
            skip_group_check=True))
        base_t = op("tensor", t_c, lambda e, k=k, pp=pp: e.matmul(
            out=PSS[:, pp:pp + 96], lhsT=S0[:, :],
            rhs=PRJ[:, 6144 + 96 * k:6144 + 96 * k + 96], start=True, stop=True,
            skip_group_check=True))
        w("vector", t_c, base_t)
        op("vector", v_c, lambda e, pp=pp: e.tensor_copy(
            out=PRE32[:, 0:32], in_=PSS[:, pp:pp + 32]))
        op("vector", v_c, lambda e, pp=pp: e.scalar_tensor_tensor(
            out=PRE32[:, 0:32], in0=PSF[:, pp:pp + 32], scalar=1.0,
            in1=PRE32[:, 0:32], op0=ALU.mult, op1=ALU.add))
        op("vector", v_c, lambda e, t0=t0: e.tensor_tensor(
            out=WORK[:, 2 * c:3 * c], in0=PRE32[:, 0:32],
            in1=VG[:, t0:t0 + 32], op=ALU.subtract))
        op("vector", v_c, lambda e, pp=pp, t0=t0: e.tensor_tensor(
            out=WORK[:, 0:c], in0=PSF[:, pp + 32:pp + 64],
            in1=VG[:, t0:t0 + 32], op=ALU.subtract))
        op("vector", v_c, lambda e, pp=pp, t0=t0: e.tensor_tensor(
            out=WORK[:, 4 * c:5 * c], in0=PSS[:, pp + 32:pp + 64],
            in1=VG[:, t0:t0 + 32], op=ALU.subtract))
        op("vector", v_c, lambda e: e.memset(SVEC[:, :], 1.0))
        w("vector", dga, ga_done)
        v0 = v_g.n
        for i in (0, 1):
            op("vector", v_g, lambda e, i=i, gb=gb: e.scalar_tensor_tensor(
                out=GROW[0:1, B7 * (i % 2):B7 * (i % 2) + B7]
                .rearrange("p (a b) -> p a b", a=7),
                in0=GALLE[0:1, gb + B7 * i:gb + B7 * i + B7]
                .rearrange("p (a b) -> p a b", a=7),
                scalar=1.0,
                in1=SVEC[0:1, :].unsqueeze(1).to_broadcast([1, 7, c]),
                op0=ALU.mult, op1=ALU.mult))

        for i in range(c):
            gstep = c * k + i
            bc = BCB[i % 2]
            w("tensor", v_g, v0 + i + 1)
            if gstep >= 2:
                w("tensor", v_w, 2 * (gstep - 1))
            if i > 0:
                op("tensor", None, lambda e, bc=bc, pi=i % 2: e.matmul(
                    out=bc[:, 0:B7], lhsT=ONESRB[0:1, :],
                    rhs=GROW[0:1, B7 * pi:B7 * pi + B7],
                    start=True, stop=False, skip_group_check=True))
                w("tensor", s_d, s_d.n)
                sbb = SVEC[0:1, i - 1:i].to_broadcast([1, 128])
                op("tensor", None, lambda e, bc=bc, sbb=sbb, gdb=gdb, i=i: e.matmul(
                    out=bc[:, i - 1:2 * c + i:2 * c], lhsT=sbb,
                    rhs=GDIAG[0:1, gdb + 4 * i:gdb + 4 * i + 2],
                    start=False, stop=False, skip_group_check=True))
                op("tensor", t_bc, lambda e, bc=bc, sbb=sbb, gdb=gdb, i=i: e.matmul(
                    out=bc[:, 4 * c + i - 1:5 * c + i:c], lhsT=sbb,
                    rhs=GDIAG[0:1, gdb + 4 * i + 2:gdb + 4 * i + 4],
                    start=False, stop=True, skip_group_check=True))
            else:
                op("tensor", t_bc, lambda e, bc=bc, pi=i % 2: e.matmul(
                    out=bc[:, 0:B7], lhsT=ONESRB[0:1, :],
                    rhs=GROW[0:1, B7 * pi:B7 * pi + B7],
                    start=True, stop=True, skip_group_check=True))
            w("vector", t_bc, t_bc.n)
            op("vector", None, lambda e, bc=bc: e.scalar_tensor_tensor(
                out=JUNKV[:, 0:3 * c], in0=WORK[:, c:4 * c], scalar=1.0,
                in1=bc[:, 0:3 * c], op0=ALU.mult, op1=ALU.mult,
                accum_out=DCOL[:, 0:1]))
            op("vector", v_w, lambda e, bc=bc, i=i: e.scalar_tensor_tensor(
                out=JUNKV[:, 0:2 * c], in0=WORK[:, 0:2 * c], scalar=1.0,
                in1=bc[:, 3 * c:5 * c], op0=ALU.mult, op1=ALU.mult,
                accum_out=WORK[:, c + i:c + i + 1]))
            op("vector", v_w, lambda e, bc=bc, i=i: e.scalar_tensor_tensor(
                out=JUNKV[:, 0:2 * c], in0=WORK[:, 3 * c:5 * c], scalar=1.0,
                in1=bc[:, 5 * c:7 * c], op0=ALU.mult, op1=ALU.mult,
                accum_out=WORK[:, 3 * c + i:3 * c + i + 1]))
            if i + 2 < c:
                w("vector", s_d, s_d.n + 1)
                op("vector", v_g, lambda e, i=i, gb=gb: e.scalar_tensor_tensor(
                    out=GROW[0:1, B7 * (i % 2):B7 * (i % 2) + B7]
                    .rearrange("p (a b) -> p a b", a=7),
                    in0=GALLE[0:1, gb + B7 * (i + 2):gb + B7 * (i + 2) + B7]
                    .rearrange("p (a b) -> p a b", a=7),
                    scalar=1.0,
                    in1=SVEC[0:1, :].unsqueeze(1).to_broadcast([1, 7, c]),
                    op0=ALU.mult, op1=ALU.mult))
            w("tensor", v_w, v_w.n)
            op("tensor", t_e, lambda e, gstep=gstep: e.matmul(
                out=MISC[0:1, 384 + gstep % 8:384 + gstep % 8 + 1],
                lhsT=DCOL[:, 0:1], rhs=DCOL[:, 0:1], start=True, stop=True,
                skip_group_check=True))
            w("scalar", t_e, t_e.n)
            op("scalar", s_d, lambda e, gstep=gstep, i=i: e.activation(
                out=SVEC[0:1, i:i + 1],
                in_=MISC[0:1, 384 + gstep % 8:384 + gstep % 8 + 1],
                func=AF.Sigmoid, scale=float(1.0 / (1.0 + 1e-6))))

        w("vector", s_d, s_d.n)
        op("vector", v_c, lambda e, t0=t0: e.tensor_copy(
            out=SROW[0:1, t0:t0 + c], in_=SVEC[0:1, :]))
        op("vector", v_c, lambda e: e.tensor_copy(out=SVF32[0:1, :],
                                                  in_=SVEC[0:1, :]))
        op("vector", v_c, lambda e, t0=t0: e.tensor_copy(
            out=MCP[:, 0:32], in_=MFG[:, t0:t0 + 32]))
        vpre = op("vector", v_c, lambda e, t0=t0: e.tensor_copy(
            out=MCP[:, 32:64], in_=MSG[:, t0:t0 + 32]))
        w("tensor", v_c, vpre)
        op("tensor", t_c, lambda e: e.transpose(
            out=MISC[0:32, 480:481], in_=SVF32[0:1, :], identity=IDT[0:1, 0:1]))
        op("tensor", t_c, lambda e: e.transpose(
            out=MISC[0:32, 192:320], in_=WORK[:, c:2 * c], identity=IDT[:, :]))
        op("tensor", t_c, lambda e: e.transpose(
            out=MISC[0:32, 320:448], in_=WORK[:, 3 * c:4 * c], identity=IDT[:, :]))
        op("tensor", t_c, lambda e: e.transpose(
            out=AUX[0:32, 0:128], in_=MCP[:, 0:32], identity=IDT[:, :]))
        tr_done = op("tensor", t_c, lambda e: e.transpose(
            out=AUX[0:32, 128:256], in_=MCP[:, 32:64], identity=IDT[:, :]))
        w("vector", t_c, tr_done)
        op("vector", v_c, lambda e: e.tensor_scalar(
            out=WTRF[:, :], in0=MISC[0:32, 192:320], scalar1=MISC[0:32, 480:481],
            scalar2=-1.0, op0=ALU.mult, op1=ALU.mult))
        op("vector", v_c, lambda e: e.tensor_scalar(
            out=WTRS[:, :], in0=MISC[0:32, 320:448], scalar1=MISC[0:32, 480:481],
            scalar2=-1.0, op0=ALU.mult, op1=ALU.mult))
        op("vector", v_c, lambda e: e.tensor_copy(out=MTF[:, :], in_=AUX[0:32, 0:128]))
        vmt = op("vector", v_c, lambda e: e.tensor_copy(out=MTS[:, :],
                                                        in_=AUX[0:32, 128:256]))
        w("tensor", v_c, vmt)
        op("tensor", t_c, lambda e, pp=pp: e.matmul(
            out=PSF[:, pp + 64:pp + 96], lhsT=WTRF[:, :], rhs=GBJF[:, :],
            start=False, stop=True, skip_group_check=True))
        op("tensor", t_c, lambda e, pp=pp: e.matmul(
            out=PSS[:, pp + 64:pp + 96], lhsT=WTRS[:, :], rhs=GBJS[:, :],
            start=False, stop=True, skip_group_check=True))
        op("tensor", t_c, lambda e: e.matmul(
            out=P5[:, 0:128], lhsT=MTF[:, :], rhs=WTRF[:, :],
            start=True, stop=True, skip_group_check=True))
        reb_t = op("tensor", t_c, lambda e: e.matmul(
            out=P6[:, 0:128], lhsT=MTS[:, :], rhs=WTRS[:, :],
            start=True, stop=True, skip_group_check=True))
        w("vector", t_c, reb_t)
        op("vector", v_c, lambda e, pp=pp, t0=t0: e.tensor_copy(
            out=OUTF[:, t0:t0 + 32], in_=PSF[:, pp + 64:pp + 96]))
        op("vector", v_c, lambda e, pp=pp, t0=t0: e.tensor_copy(
            out=OUTS[:, t0:t0 + 32], in_=PSS[:, pp + 64:pp + 96]))
        op("vector", v_c, lambda e: e.scalar_tensor_tensor(
            out=REBT[:, :], in0=P5[:, 0:128], scalar=1.0, in1=F0[:, :],
            op0=ALU.mult, op1=ALU.add))
        op("vector", v_c, lambda e, k=k: e.tensor_scalar(
            out=F0[:, :], in0=REBT[:, :], scalar1=DLF[:, k:k + 1], scalar2=None,
            op0=ALU.mult))
        op("vector", v_c, lambda e: e.scalar_tensor_tensor(
            out=REBT[:, :], in0=P6[:, 0:128], scalar=1.0, in1=S0[:, :],
            op0=ALU.mult, op1=ALU.add))
        op("vector", v_c, lambda e, k=k: e.tensor_scalar(
            out=S0[:, :], in0=REBT[:, :], scalar1=DLS[:, k:k + 1], scalar2=None,
            op0=ALU.mult))
        if k == 0:
            op("vector", v_c, lambda e: e.tensor_copy(out=DBGT[:, 288:320],
                                                      in_=WORK[:, 2 * c:3 * c]))
            op("vector", v_c, lambda e: e.tensor_copy(out=DBGT[:, 320:352],
                                                      in_=WORK[:, c:2 * c]))
            op("vector", v_c, lambda e: e.tensor_copy(out=DBGT[0:1, 352:384],
                                                      in_=SVF32[0:1, :]))
            op("vector", v_c, lambda e: e.tensor_copy(out=DBGT[:, 384:416],
                                                      in_=OUTF[:, 0:32]))
            op("vector", v_c, lambda e: e.tensor_copy(out=DBGT[:, 416:448],
                                                      in_=OUTS[:, 0:32]))
            op("vector", v_c, lambda e: e.tensor_copy(out=DBGT[:, 448:480],
                                                      in_=F0[:, 0:32]))
        if k == 4:
            op("vector", v_c, lambda e: e.tensor_copy(out=DBGT[:, 896:928],
                                                      in_=WORK[:, 2 * c:3 * c]))
            op("vector", v_c, lambda e: e.tensor_copy(out=DBGT[:, 928:960],
                                                      in_=WORK[:, c:2 * c]))
            op("vector", v_c, lambda e: e.tensor_copy(out=DBGT[0:1, 960:992],
                                                      in_=SVF32[0:1, :]))
            op("vector", v_c, lambda e: e.tensor_copy(out=DBGT[:, 992:1024],
                                                      in_=OUTF[:, 128:160]))

    op("vector", v_c, lambda e: e.tensor_scalar(
        out=ROWSB[0:1, 0:2048], in0=SROW[0:1, :], scalar1=0.7, scalar2=None,
        op0=ALU.is_gt))
    op("vector", v_c, lambda e: e.tensor_scalar(
        out=SROW[0:1, :], in0=ROWSB[0:1, 0:2048], scalar1=-0.1,
        scalar2=1.0, op0=ALU.mult, op1=ALU.add))
    op("vector", v_c, lambda e: e.tensor_scalar(
        out=ROWSB[0:1, 0:2048], in0=ROWSB[0:1, 0:2048], scalar1=0.1,
        scalar2=None, op0=ALU.mult))
    for q in range(4):
        w("tensor", v_c)
        op("tensor", t_c, lambda e, q=q: e.matmul(
            out=MISC[:, 0:512], lhsT=ONESRB[0:1, :],
            rhs=SROW[0:1, 512 * q:512 * q + 512], start=True,
            stop=True, skip_group_check=True))
        op("tensor", t_c, lambda e, q=q: e.matmul(
            out=AUX[:, 0:512], lhsT=ONESRB[0:1, :],
            rhs=ROWSB[0:1, 512 * q:512 * q + 512], start=True, stop=True,
            skip_group_check=True))
        w("vector", t_c)
        op("vector", v_c, lambda e, q=q: e.tensor_copy(
            out=XN[:, 512 * q:512 * q + 512], in_=MISC[:, 0:512]))
        op("vector", v_c, lambda e, q=q: e.tensor_tensor(
            out=XN[:, 2048 + 512 * q:2048 + 512 * q + 512],
            in0=NUG[:, 512 * q:512 * q + 512], in1=AUX[:, 0:512], op=ALU.mult))
    op("vector", v_c, lambda e: e.tensor_tensor_scan(
        out=NSG[:, :], data0=XN[:, 0:2048], data1=XN[:, 2048:4096],
        initial=0.0, op0=ALU.mult, op1=ALU.add))
    op("vector", v_c, lambda e: e.tensor_copy(out=DBGT[:, 680:808],
                                              in_=NSG[:, 128:256]))
    w("sync", v_c)
    won_ld = dma("sync", din, NUG[:, :], wos[:, :])

    for tb in range(NTB):
        tc0 = TB * tb
        for i in range(NI):
            dma("gpsimd", dx1, X1B[:, 512 * i:512 * i + 512],
                x1s[128 * i:128 * i + 128, tc0:tc0 + 512])
        x1_ld = dx1.n
        for t4 in range(4):
            tt = 4 * tb + t4
            t0 = 128 * tt
            w("tensor", v_c)
            op("tensor", t_c, lambda e, t0=t0: e.transpose(
                out=MISC[:, 0:128], in_=OUTF[:, t0:t0 + 128], identity=IDT[:, :]))
            op("tensor", t_c, lambda e, t0=t0: e.transpose(
                out=MISC[:, 128:256], in_=OUTS[:, t0:t0 + 128], identity=IDT[:, :]))
            w("vector", t_c, t_c.n - 2)
            op("vector", v_c, lambda e, t0=t0: e.tensor_copy(
                out=OTC[:, :], in_=NSG[:, t0:t0 + 128]))
            w("tensor", v_c)
            op("tensor", t_c, lambda e: e.transpose(
                out=MISC[:, 256:384], in_=OTC[:, :], identity=IDT[:, :]))
            w("vector", t_c)
            if tb == 0 and t4 == 1:
                op("vector", v_c, lambda e: e.tensor_copy(
                    out=DBGT[:, 128:256], in_=MISC[:, 0:128]))
                op("vector", v_c, lambda e: e.tensor_copy(
                    out=DBGT[:, 256:384], in_=MISC[:, 128:256]))
                op("vector", v_c, lambda e: e.tensor_copy(
                    out=DBGT[:, 384:512], in_=MISC[:, 256:384]))
            op("vector", v_c, lambda e, tt=tt: e.tensor_scalar(
                out=XBLK[:, 4096:4224], in0=MISC[:, 0:128],
                scalar1=PWT[:, 3 * tt:3 * tt + 1], scalar2=None, op0=ALU.mult))
            op("vector", v_c, lambda e, tt=tt: e.tensor_scalar(
                out=XBLK[:, 4224:4352], in0=MISC[:, 128:256],
                scalar1=PWT[:, 3 * tt + 1:3 * tt + 2], scalar2=None, op0=ALU.mult))
            op("vector", v_c, lambda e: e.tensor_tensor(
                out=XBLK[:, 4096:4224], in0=XBLK[:, 4096:4224],
                in1=XBLK[:, 4224:4352], op=ALU.add))
            op("vector", v_c, lambda e, tt=tt: e.tensor_scalar(
                out=XBLK[:, 4224:4352], in0=MISC[:, 256:384],
                scalar1=PWT[:, 3 * tt + 2:3 * tt + 3], scalar2=None, op0=ALU.mult))
            op("vector", v_c, lambda e: e.tensor_tensor(
                out=XBLK[:, 4096:4224], in0=XBLK[:, 4096:4224],
                in1=XBLK[:, 4224:4352], op=ALU.add))
            if tb == 0 and t4 == 1:
                op("vector", v_c, lambda e: e.tensor_copy(
                    out=DBGT[:, 0:128], in_=XBLK[:, 4096:4224]))
            op("vector", v_c, lambda e: e.tensor_tensor(
                out=XBLK[:, 4224:4352], in0=XBLK[:, 4096:4224],
                in1=XBLK[:, 4096:4224], op=ALU.mult))
            op("vector", v_c, lambda e: e.tensor_reduce(
                out=SCL[:, 4:5], in_=XBLK[:, 4224:4352],
                axis=mybir.AxisListType.X, op=ALU.add))
            op("vector", v_c, lambda e: e.memset(VABS[:, 4:8], 0.0))
            w("scalar", v_c)
            op("scalar", s_c, lambda e: e.activation(
                out=SCL[:, 5:6], in_=SCL[:, 4:5], func=AF.Sqrt,
                bias=RMS_EPS, scale=1.0 / 128.0))
            w("vector", s_c)
            op("vector", v_c, lambda e: e.reciprocal(
                out=SCL[:, 6:7], in_=SCL[:, 5:6]))
            op("vector", v_c, lambda e: e.memset(VABS[:, 4:8], 0.0))
            op("vector", v_c, lambda e: e.tensor_scalar(
                out=OTC[:, :], in0=XBLK[:, 4096:4224], scalar1=SCL[:, 6:7],
                scalar2=None, op0=ALU.mult))
            if tb == 0 and t4 == 1:
                op("vector", v_c, lambda e: e.tensor_copy(
                    out=DBGT[:, 512:515], in_=PWT[:, 3:6]))
                op("vector", v_c, lambda e: e.tensor_copy(
                    out=DBGT[:, 515:516], in_=SCL[:, 4:5]))
                op("vector", v_c, lambda e: e.tensor_copy(
                    out=DBGT[:, 516:517], in_=SCL[:, 6:7]))
                op("vector", v_c, lambda e: e.tensor_copy(
                    out=DBGT[:, 520:648], in_=OTC[:, :]))
            w("tensor", v_c)
            op("tensor", t_c, lambda e: e.transpose(
                out=MISC[:, 0:128], in_=OTC[:, :], identity=IDT[:, :]))
            w("vector", t_c)
            op("vector", v_c, lambda e, t4=t4: e.tensor_copy(
                out=KG[:, 128 * t4:128 * t4 + 128], in_=MISC[:, 0:128]))
        otv_done = v_c.n
        w("vector", dx1, x1_ld)
        w("tensor", din, won_ld)
        for o in range(NO):
            par = o % 2
            w("tensor", v_c)
            op("tensor", t_c, lambda e, o=o, par=par: e.matmul(
                out=PB[5 + par][:, 0:512], lhsT=NUG[:, 128 * o:128 * o + 128],
                rhs=KG[:, 0:512], start=True, stop=True, skip_group_check=True))
            w("vector", t_c)
            op("vector", v_c, lambda e, o=o, par=par: e.tensor_tensor(
                out=XBLK[:, 512 * o:512 * o + 512], in0=PB[5 + par][:, 0:512],
                in1=X1B[:, 512 * o:512 * o + 512], op=ALU.add))
            if tb == 0 and o == 0:
                op("vector", v_c, lambda e: e.tensor_copy(
                    out=DBGT[:, 648:680], in_=XBLK[:, 128:160]))
                op("vector", v_c, lambda e: e.tensor_copy(
                    out=DBGT[:, 808:840], in_=X1B[:, 128:160]))
        emit_ln(XBLK)
        emit_ffn(XBLK, X1B, yT, tc0)

    w("sync", v_c)
    dma("sync", dst, dbg_out, DBGT[:, :])
    w("sync", dx1, dx1.n)
    w("sync", dst, dst.n)

    with nc.Block() as block:
        @block.sync
        def _(e):
            for f in ops["sync"]:
                f(e)

        @block.vector
        def _(e):
            for f in ops["vector"]:
                f(e)

        @block.scalar
        def _(e):
            for f in ops["scalar"]:
                f(e)

        @block.tensor
        def _(e):
            for f in ops["tensor"]:
                f(e)

        @block.gpsimd
        def _(e):
            for f in ops["gpsimd"]:
                f(e)

    for m in reversed(stack):
        m.__exit__(None, None, None)
    return nc


_NC_CACHE = {}


def _consts():
    ident = np.eye(128, dtype=np.float32)
    us = np.zeros((32, 32), np.float32)
    ui = np.zeros((32, 32), np.float32)
    for t in range(32):
        for i in range(32):
            us[t, i] = 1.0 if t < i else 0.0
            ui[t, i] = 1.0 if t <= i else 0.0
    u = np.concatenate([us, ui], axis=1).astype(np.float32)
    maskak = np.zeros((64, 32), BF)
    for i in range(32):
        for j in range(32):
            v = -1.0 if j <= i - 2 else 0.0
            maskak[i, j] = BF(v)
            maskak[32 + i, j] = BF(v)
    maskb = np.zeros((32, 32), BF)
    for j in range(32):
        for i in range(32):
            maskb[j, i] = BF(1.0 if j <= i else 0.0)
    galle = np.zeros((1, c * B7), BF)
    for i in range(c):
        for blk in (1, 3, 6):
            galle[0, B7 * i + blk * c + i] = BF(1.0)
    return ident, u, maskak, maskb, galle


def kernel(L0_attn_x, L0_attn_Wk, L0_attn_Wv, L0_attn_Wq, L0_attn_fast_lambda,
           L0_attn_slow_lambda, L0_attn_Waf, L0_attn_baf, L0_attn_Was,
           L0_attn_bas, L0_attn_nm_w1, L0_attn_nm_b1, L0_attn_nm_w2,
           L0_attn_nm_b2, L0_attn_ir_w1, L0_attn_ir_b1, L0_attn_ir_w2,
           L0_attn_ir_b2, L0_attn_pw_w, L0_attn_pw_b, L0_attn_Wo,
           L0_attn_norm_w, L0_ffn_weight, L0_ffn_bias):
    x = np.ascontiguousarray(np.asarray(L0_attn_x, np.float32))
    B = x.shape[0]
    ident, u, maskak, maskb, galle = _consts()

    projcat = np.zeros((D, 896), np.float32)
    projcat[:, 0:128] = np.asarray(L0_attn_Wk, np.float32)
    projcat[:, 128:256] = np.asarray(L0_attn_Wv, np.float32)
    projcat[:, 256:384] = np.asarray(L0_attn_Wq, np.float32)
    projcat[:, 384:512] = np.asarray(L0_attn_Waf, np.float32)
    projcat[:, 512:640] = np.asarray(L0_attn_Was, np.float32)
    projcat[:, 640:768] = np.asarray(L0_attn_ir_w1, np.float32)
    projcat[:, 768:771] = np.asarray(L0_attn_pw_w, np.float32)

    smallv = np.zeros((128, 16), np.float32)
    smallv[:, 0] = np.asarray(L0_attn_fast_lambda, np.float32)
    smallv[:, 1] = np.asarray(L0_attn_slow_lambda, np.float32)
    smallv[:, 2] = np.asarray(L0_attn_baf, np.float32)
    smallv[:, 3] = np.asarray(L0_attn_bas, np.float32)
    smallv[:, 4] = np.asarray(L0_attn_nm_b2, np.float32)
    smallv[:, 5] = np.asarray(L0_attn_ir_b1, np.float32)
    smallv[:, 6] = np.asarray(L0_attn_norm_w, np.float32)
    smallv[:, 7] = np.asarray(L0_attn_ir_w2, np.float32)[:, 0]
    smallv[0, 8] = np.float32(np.asarray(L0_attn_ir_b2, np.float32).reshape(-1)[0])
    smallv[0:3, 9] = np.asarray(L0_attn_pw_b, np.float32)

    nmb1 = np.zeros((128, 2), np.float32)
    nmb1[:, 0] = np.asarray(L0_attn_nm_b1, np.float32)[0:128]
    nmb1[:, 1] = np.asarray(L0_attn_nm_b1, np.float32)[128:256]
    fb = 0.5 * np.asarray(L0_ffn_bias, np.float32).reshape(16, 128).T.copy()
    wTh = np.ascontiguousarray(np.asarray(L0_ffn_weight, np.float32).T)

    if "nc" not in _NC_CACHE:
        _NC_CACHE["nc"] = build_program()
    nc = _NC_CACHE["nc"]

    in_maps = []
    for core in range(8):
        b = core % B
        in_maps.append({
            "xT": np.ascontiguousarray(x[b].T),
            "wT": wTh,
            "projcat": projcat,
            "nmw1_in": np.asarray(L0_attn_nm_w1, np.float32),
            "nmw2_in": np.asarray(L0_attn_nm_w2, np.float32),
            "wo_in": np.asarray(L0_attn_Wo, np.float32),
            "smallv": smallv,
            "nmb1_in": nmb1,
            "fbias": fb,
            "cident": ident,
            "cU": u,
            "maskak_in": maskak,
            "maskb_in": maskb,
            "galle_in": galle,
            "cbias_in": np.tile(np.array([[1e-8, np.log(0.5), 1e-5, 1e-24,
                                           1e-6, 1e-10, 0.0, 0.0]], np.float32),
                                (128, 1)),
        })

    res = run_bass_kernel_spmd(nc, in_maps, core_ids=list(range(8)))
    _NC_CACHE["res"] = res
    out = np.empty((B, T, D), np.float32)
    for b in range(B):
        out[b] = np.ascontiguousarray(res.results[b]["yT"].T)
    return out


# version 03
# speedup vs baseline: 1.0342x; 1.0342x over previous
import numpy as np
import ml_dtypes
import concourse.bass as bass
import concourse.mybir as mybir
from concourse.bass_utils import run_bass_kernel_spmd

F32 = mybir.dt.float32
BF16 = mybir.dt.bfloat16
AF = mybir.ActivationFunctionType
ALU = mybir.AluOpType
BF = ml_dtypes.bfloat16

D = 2048
T = 2048
c = 32
SC = T // c
NI = D // 128
NTB = 4
TB = 512
NO = D // 128

C_DECAY = 8.0
LN_EPS = 1e-5
RMS_EPS = 1e-6
MAGIC = 12582912.0
B7 = 7 * c


class Ctr:
    def __init__(self, s):
        self.sem = s
        self.n = 0


def build_program():
    nc = bass.Bass("TRN2", target_bir_lowering=False)

    xT = nc.dram_tensor("xT", [D, T], F32, kind="ExternalInput").ap()
    wT = nc.dram_tensor("wT", [D, D], F32, kind="ExternalInput").ap()
    projcat = nc.dram_tensor("projcat", [D, 896], F32, kind="ExternalInput").ap()
    nmw1_in = nc.dram_tensor("nmw1_in", [128, 256], F32, kind="ExternalInput").ap()
    nmw2_in = nc.dram_tensor("nmw2_in", [256, 128], F32, kind="ExternalInput").ap()
    wo_in = nc.dram_tensor("wo_in", [128, D], F32, kind="ExternalInput").ap()
    smallv = nc.dram_tensor("smallv", [128, 16], F32, kind="ExternalInput").ap()
    nmb1_in = nc.dram_tensor("nmb1_in", [128, 2], F32, kind="ExternalInput").ap()
    fbias = nc.dram_tensor("fbias", [128, 16], F32, kind="ExternalInput").ap()
    cident = nc.dram_tensor("cident", [128, 128], F32, kind="ExternalInput").ap()
    cU = nc.dram_tensor("cU", [32, 64], F32, kind="ExternalInput").ap()
    maskak_in = nc.dram_tensor("maskak_in", [64, 32], BF16, kind="ExternalInput").ap()
    maskb_in = nc.dram_tensor("maskb_in", [32, 32], BF16, kind="ExternalInput").ap()
    galle_in = nc.dram_tensor("galle_in", [1, c * B7], BF16, kind="ExternalInput").ap()
    cbias_in = nc.dram_tensor("cbias_in", [128, 8], F32, kind="ExternalInput").ap()
    x1s = nc.dram_tensor("x1s", [D, T], F32, kind="Internal").ap()
    wqs = nc.dram_tensor("wqs", [D, D], BF16, kind="Internal").ap()
    wos = nc.dram_tensor("wos", [128, D], BF16, kind="Internal").ap()
    yT = nc.dram_tensor("yT", [D, T], F32, kind="ExternalOutput").ap()
    dbg_out = nc.dram_tensor("dbg", [128, 128], BF16, kind="ExternalOutput").ap()

    ops = {k: [] for k in ("sync", "vector", "scalar", "tensor", "gpsimd")}
    stack = []

    def sb(name, shape, dt):
        m = nc.sbuf_tensor(name, shape, dt)
        t = m.__enter__()
        stack.append(m)
        return t

    def psum(name, dt=F32):
        m = nc.psum_tensor(name, [128, 512], dt)
        t = m.__enter__()
        stack.append(m)
        return t

    def mksem(name):
        m = nc.semaphore(name)
        s = m.__enter__()
        stack.append(m)
        return Ctr(s)

    XBLK = sb("XBLK", [128, 8192], F32)
    X1B = sb("X1B", [128, 8192], F32)
    XN = sb("XN", [128, 8192], BF16)
    SQT = sb("SQT", [128, 512], BF16)
    PWS = sb("PWS", [4, 512], F32)
    WQC = sb("WQC", [128, 4096], BF16)
    PRJ = sb("PRJ", [128, 14336], BF16)
    KG = sb("KG", [128, T], BF16)
    QG = sb("QG", [128, T], BF16)
    KSUG = sb("KSUG", [128, T], BF16)
    VG = sb("VG", [128, T], BF16)
    NUG = sb("NUG", [128, T], BF16)
    MFG = sb("MFG", [128, T], BF16)
    MSG = sb("MSG", [128, T], BF16)
    OUTF = sb("OUTF", [128, T], F32)
    OUTS = sb("OUTS", [128, T], F32)
    NSG = sb("NSG", [128, T], BF16)
    LAFG = OUTF
    LASG = OUTS
    LT32 = sb("LT32", [32, 512], F32)
    EXPT = sb("EXPT", [128, 768], BF16)
    DLF = sb("DLF", [128, 64], F32)
    DLS = sb("DLS", [128, 64], F32)
    SROW = sb("SROW", [1, T], BF16)
    PWT = sb("PWT", [128, 48], F32)
    GALLE = sb("GALLE", [1, c * B7], BF16)
    GDIAG = sb("GDIAG", [1, 2 * 4 * c], BF16)
    GAKF = sb("GAKF", [64, 32], BF16)
    GAKS = sb("GAKS", [64, 32], BF16)
    GBJF = sb("GBJF", [32, 32], BF16)
    GBJS = sb("GBJS", [32, 32], BF16)
    WTRF = sb("WTRF", [32, 128], BF16)
    WTRS = sb("WTRS", [32, 128], BF16)
    MTF = sb("MTF", [32, 128], BF16)
    MTS = sb("MTS", [32, 128], BF16)
    MCP = sb("MCP", [128, 64], F32)
    WORK = sb("WORK", [128, 5 * c], F32)
    SVEC = sb("SVEC", [1, c], BF16)
    SVF32 = sb("SVF32", [1, c], F32)
    GROW = sb("GROW", [1, 2 * B7], BF16)
    DCOL = sb("DCOL", [128, 2], F32)
    JUNKV = sb("JUNKV", [128, 3 * c], F32)
    PRE32 = sb("PRE32", [128, 32], F32)
    F0 = sb("F0", [128, 128], BF16)
    S0 = sb("S0", [128, 128], BF16)
    REBT = sb("REBT", [128, 128], F32)
    OTC = REBT
    IDT = sb("IDT", [128, 128], F32)
    U32 = sb("U32", [32, 64], F32)
    MASKAK = sb("MASKAK", [64, 32], BF16)
    GDALL2 = sb("GDALL2", [128, 64], F32)
    MASKB = sb("MASKB", [32, 32], BF16)
    ONES = sb("ONES", [128, 4], F32)
    ONESB = sb("ONESB", [128, 4], BF16)
    ONESRB = sb("ONESRB", [1, 128], BF16)
    ONESRF = sb("ONESRF", [1, 128], F32)
    SMALL = sb("SMALL", [128, 16], F32)
    SMALLB = sb("SMALLB", [128, 8], BF16)
    NMB1 = sb("NMB1", [128, 2], F32)
    FBC = sb("FBC", [128, 16], F32)
    NMW1 = sb("NMW1", [128, 256], BF16)
    NMW2 = sb("NMW2", [128, 256], BF16)
    LSF = sb("LSF", [128, 4], F32)
    ROWS = sb("ROWS", [1, 1024], F32)
    ROWSB = sb("ROWSB", [1, 2048], BF16)
    SCL = sb("SCL", [128, 8], F32)
    VABS = sb("VABS", [128, 36], F32)
    CB = sb("CB", [128, 8], F32)
    DBGT = sb("DBGT", [128, 128], BF16)

    PB = [psum(f"PB{i}") for i in range(8)]
    BCB = [PB[0], PB[1]]
    MISC = PB[2]
    PSF = PB[3]
    PSS = PB[4]
    P5 = PB[5]
    P6 = PB[6]
    AUX = PB[7]

    din = mksem("din")
    dst = mksem("dst")
    dwq = mksem("dwq")
    dx1 = mksem("dx1")
    dga = mksem("dga")
    v_c = mksem("v_c")
    t_c = mksem("t_c")
    s_c = mksem("s_c")
    v_g = mksem("v_g")
    t_bc = mksem("t_bc")
    v_w = mksem("v_w")
    t_e = mksem("t_e")
    s_d = mksem("s_d")

    CBVALS = [1e-8, float(np.log(0.5)), LN_EPS, 1e-24, RMS_EPS, 1e-10]

    def w(eng, ctr, val=None):
        v = ctr.n if val is None else val
        s = ctr.sem
        ops[eng].append(lambda e: e.wait_ge(s, v))

    def dma(eng, ctr, out, in_):
        ctr.n += 16
        s = ctr.sem
        ops[eng].append(lambda e: e.dma_start(out=out, in_=in_).then_inc(s, 16))
        return ctr.n

    def op(eng, ctr, fn):
        if ctr is not None:
            ctr.n += 1
            s = ctr.sem
            ops[eng].append(lambda e: fn(e).then_inc(s, 1))
            return ctr.n
        ops[eng].append(lambda e: fn(e))
        return None

    for ci, cv in enumerate(CBVALS):
        nc.const_aps.register(F32, cv, CB[:, ci:ci + 1])

    dma("sync", din, IDT[:, :], cident)
    dma("sync", din, U32[:, :], cU)
    dma("sync", din, MASKAK[:, :], maskak_in)
    dma("sync", din, MASKB[:, :], maskb_in)
    dma("sync", din, SMALL[:, :], smallv)
    dma("sync", din, NMB1[:, :], nmb1_in)
    dma("sync", din, FBC[:, :], fbias)
    dma("sync", din, GALLE[0:1, 0:c * B7], galle_in)
    dma("sync", din, CB[:, :], cbias_in)
    dma("sync", din, XBLK[:, 0:256], nmw1_in)
    dma("sync", din, XBLK[:, 256:384], nmw2_in[0:128, :])
    dma("sync", din, XBLK[:, 384:512], nmw2_in[128:256, :])
    dma("sync", din, XBLK[:, 512:2560], wo_in)
    n_init = din.n

    w("vector", din, n_init)
    for fn in (
        lambda e: e.memset(ONES[:, :], 1.0),
        lambda e: e.memset(ONESB[:, :], 1.0),
        lambda e: e.memset(ONESRB[:, :], 1.0),
        lambda e: e.memset(ONESRF[:, :], 1.0),
        lambda e: e.memset(WORK[:, :], 0.0),
        lambda e: e.memset(GDIAG[:, :], 0.0),
        lambda e: e.memset(GROW[:, :], 0.0),
        lambda e: e.memset(DCOL[:, :], 0.0),
        lambda e: e.memset(F0[:, :], 0.0),
        lambda e: e.memset(S0[:, :], 0.0),
        lambda e: e.memset(DBGT[:, :], 0.0),
        lambda e: e.memset(VABS[:, :], 0.0),
        lambda e: e.memset(JUNKV[:, :], 0.0),
        lambda e: e.tensor_copy(out=NMW1[:, :], in_=XBLK[:, 0:256]),
        lambda e: e.tensor_copy(out=NMW2[:, :], in_=XBLK[:, 256:512]),
        lambda e: e.tensor_copy(out=SMALLB[:, 0:1], in_=SMALL[:, 7:8]),
    ):
        op("vector", v_c, fn)
    for j in range(4):
        op("vector", v_c, lambda e, j=j: e.tensor_scalar(
            out=NSG[:, 512 * j:512 * j + 512],
            in0=XBLK[:, 512 + 512 * j:1024 + 512 * j],
            scalar1=SMALL[:, 6:7], scalar2=None, op0=ALU.mult))
    w("gpsimd", v_c)
    dma("gpsimd", dx1, wos[:, :], NSG[:, 0:2048])
    w("scalar", din, n_init)
    op("scalar", s_c, lambda e: e.activation(out=LSF[:, 2:3], in_=SMALL[:, 0:1],
                                             func=AF.Sigmoid))
    op("scalar", s_c, lambda e: e.activation(out=LSF[:, 3:4], in_=SMALL[:, 1:2],
                                             func=AF.Sigmoid))
    op("scalar", s_c, lambda e: e.activation(out=LSF[:, 0:1], in_=LSF[:, 2:3],
                                             func=AF.Ln, bias=1e-8, scale=1.0))
    op("scalar", s_c, lambda e: e.activation(out=LSF[:, 1:2], in_=LSF[:, 3:4],
                                             func=AF.Ln, bias=1e-8, scale=1.0))
    w("vector", s_c)
    op("vector", v_c, lambda e: e.tensor_scalar(out=LSF[:, 0:2], in0=LSF[:, 0:2],
                                                scalar1=C_DECAY, scalar2=None,
                                                op0=ALU.mult))
    for i in range(NI):
        off = 896 * (i % 2)
        tgt = dma("sync", din, X1B[:, off:off + 896],
                  projcat[128 * i:128 * i + 128, :])
        w("vector", din, tgt)
        op("vector", v_c, lambda e, i=i, off=off: e.tensor_copy(
            out=PRJ[:, 896 * i:896 * i + 896], in_=X1B[:, off:off + 896]))
        if i >= 1:
            w("sync", v_c)

    abs_marks = []
    for it in range(32):
        r0 = 128 * (it // 2)
        cw = 1024 * (it % 2)
        lo = 2560 + 1024 * (it % 4)
        if it >= 4:
            w("sync", s_c, abs_marks[it - 4])
        tgt = dma("sync", din, XBLK[:, lo:lo + 1024], wT[r0:r0 + 128, cw:cw + 1024])
        w("scalar", din, tgt)
        mk = op("scalar", s_c, lambda e, it=it, lo=lo: e.activation(
            out=X1B[:, 4096:5120], in_=XBLK[:, lo:lo + 1024],
            func=AF.Abs, accum_out=VABS[:, it + 1:it + 2]))
        abs_marks.append(mk)
    w("vector", s_c)
    op("vector", v_c, lambda e: e.tensor_reduce(out=VABS[:, 0:1], in_=VABS[:, 1:33],
                                                axis=mybir.AxisListType.X, op=ALU.add))
    op("vector", v_c, lambda e: e.memset(VABS[:, 33:36], 0.0))
    w("tensor", din, n_init)
    w("tensor", v_c)
    op("tensor", t_c, lambda e: e.matmul(out=MISC[0:1, 0:1], lhsT=VABS[:, 0:1],
                                         rhs=ONES[:, 0:1], start=True, stop=True,
                                         skip_group_check=True))
    w("scalar", t_c)
    op("scalar", s_c, lambda e: e.activation(out=SCL[0:1, 4:5], in_=MISC[0:1, 0:1],
                                             func=AF.Copy, scale=1.0 / (D * D),
                                             bias=1e-10))
    op("scalar", s_c, lambda e: e.activation(out=SCL[0:1, 1:2], in_=MISC[0:1, 0:1],
                                             func=AF.Copy, scale=0.5 / (D * D)))
    w("vector", s_c)
    op("vector", v_c, lambda e: e.reciprocal(out=SCL[0:1, 0:1], in_=SCL[0:1, 4:5]))
    op("vector", v_c, lambda e: e.memset(VABS[:, 4:8], 0.0))
    w("tensor", s_c)
    w("tensor", v_c)
    op("tensor", t_c, lambda e: e.matmul(out=MISC[0:128, 8:10], lhsT=ONESRF[0:1, :],
                                         rhs=SCL[0:1, 0:2], start=True, stop=True,
                                         skip_group_check=True))
    w("vector", t_c)
    op("vector", v_c, lambda e: e.tensor_copy(out=SCL[:, 2:4], in_=MISC[:, 8:10]))
    op("vector", v_c, lambda e: e.memset(VABS[:, 4:8], 0.0))
    op("vector", v_c, lambda e: e.memset(VABS[:, 8:12], 0.0))
    op("vector", v_c, lambda e: e.tensor_scalar(
        out=SCL[:, 5:6], in0=SCL[:, 2:3], scalar1=-1.0, scalar2=None,
        op0=ALU.mult))
    op("vector", v_c, lambda e: e.memset(VABS[:, 4:8], 0.0))
    op("vector", v_c, lambda e: e.memset(VABS[:, 8:12], 0.0))
    store_marks = []
    for it in range(32):
        r0 = 128 * (it // 2)
        cw = 1024 * (it % 2)
        lo = 2560 + 1024 * (it % 4)
        xo = 1024 * (it % 4)
        if it >= 4:
            w("sync", v_c)
        tgt = dma("sync", din, XBLK[:, lo:lo + 1024], wT[r0:r0 + 128, cw:cw + 1024])
        w("vector", din, tgt)
        op("vector", v_c, lambda e, lo=lo, xo=xo: e.tensor_scalar(
            out=X1B[:, xo:xo + 1024], in0=XBLK[:, lo:lo + 1024],
            scalar1=SCL[:, 2:3], scalar2=0.5, op0=ALU.mult, op1=ALU.is_gt))
        op("vector", v_c, lambda e, lo=lo, xo=xo: e.tensor_scalar(
            out=X1B[:, 4096 + xo:4096 + xo + 1024], in0=XBLK[:, lo:lo + 1024],
            scalar1=SCL[:, 5:6], scalar2=0.5, op0=ALU.mult, op1=ALU.is_gt))
        if it >= 4:
            w("vector", dx1, store_marks[it - 4])
        op("vector", v_c, lambda e, xo=xo: e.tensor_tensor(
            out=XN[:, xo:xo + 1024], in0=X1B[:, xo:xo + 1024],
            in1=X1B[:, 4096 + xo:4096 + xo + 1024], op=ALU.subtract))
        w("gpsimd", v_c)
        store_marks.append(dma("gpsimd", dx1, wqs[r0:r0 + 128, cw:cw + 1024],
                               XN[:, xo:xo + 1024]))
    wq_done = dx1.n

    def emit_ln(SRC):
        for i in range(NI):
            co = 512 * i
            op("vector", v_c, lambda e, co=co: e.tensor_copy(
                out=XN[:, co:co + 512], in_=SRC[:, co:co + 512]))
            op("vector", v_c, lambda e, co=co: e.scalar_tensor_tensor(
                out=SQT[:, 0:512], in0=XN[:, co:co + 512], scalar=1.0,
                in1=XN[:, co:co + 512], op0=ALU.mult, op1=ALU.mult))
            w("tensor", v_c)
            op("tensor", t_c, lambda e, i=i: e.matmul(
                out=MISC[0:1, 0:512], lhsT=ONESB[:, 0:1],
                rhs=XN[:, 512 * i:512 * i + 512],
                start=(i == 0), stop=(i == NI - 1), skip_group_check=True))
            op("tensor", t_c, lambda e, i=i: e.matmul(
                out=AUX[0:1, 0:512], lhsT=ONESB[:, 0:1], rhs=SQT[:, 0:512],
                start=(i == 0), stop=(i == NI - 1), skip_group_check=True))
        w("scalar", t_c)
        op("scalar", s_c, lambda e: e.activation(out=ROWS[0:1, 0:512],
                                                 in_=MISC[0:1, 0:512],
                                                 func=AF.Copy, scale=1.0 / D))
        w("vector", s_c)
        op("vector", v_c, lambda e: e.tensor_tensor(
            out=ROWS[0:1, 512:1024], in0=ROWS[0:1, 0:512],
            in1=ROWS[0:1, 0:512], op=ALU.mult))
        op("vector", v_c, lambda e: e.scalar_tensor_tensor(
            out=PWS[0:1, 0:512], in0=AUX[0:1, 0:512], scalar=1.0 / D,
            in1=ROWS[0:1, 512:1024], op0=ALU.mult, op1=ALU.subtract))
        w("scalar", v_c)
        op("scalar", s_c, lambda e: e.activation(out=ROWS[0:1, 512:1024],
                                                 in_=PWS[0:1, 0:512],
                                                 func=AF.Sqrt, bias=LN_EPS,
                                                 scale=1.0))
        op("scalar", s_c, lambda e: e.activation(out=ROWSB[0:1, 512:1024],
                                                 in_=ROWS[0:1, 0:512], func=AF.Copy))
        w("vector", s_c)
        op("vector", v_c, lambda e: e.reciprocal(out=ROWS[0:1, 0:512],
                                                 in_=ROWS[0:1, 512:1024]))
        op("vector", v_c, lambda e: e.memset(VABS[:, 4:8], 0.0))
        op("vector", v_c, lambda e: e.tensor_copy(out=ROWSB[0:1, 0:512],
                                                  in_=ROWS[0:1, 0:512]))
        w("tensor", s_c)
        w("tensor", v_c)
        op("tensor", t_c, lambda e: e.matmul(out=MISC[:, 0:512], lhsT=ONESRB[0:1, :],
                                             rhs=ROWSB[0:1, 512:1024], start=True,
                                             stop=True, skip_group_check=True))
        op("tensor", t_c, lambda e: e.matmul(out=AUX[:, 0:512], lhsT=ONESRB[0:1, :],
                                             rhs=ROWSB[0:1, 0:512], start=True,
                                             stop=True, skip_group_check=True))
        w("vector", t_c)
        for i in range(NI):
            co = 512 * i
            op("vector", v_c, lambda e, co=co: e.tensor_tensor(
                out=XN[:, co:co + 512], in0=SRC[:, co:co + 512],
                in1=MISC[:, 0:512], op=ALU.subtract))
            op("vector", v_c, lambda e, co=co: e.tensor_tensor(
                out=XN[:, co:co + 512], in0=XN[:, co:co + 512],
                in1=AUX[:, 0:512], op=ALU.mult))

    wqc_marks = []
    ffn_state = {"prev_store": 0}

    def emit_ffn(SRC, DSTF, store_to, tc0):
        w("tensor", v_c)
        w("vector", dx1, ffn_state["prev_store"])
        evac_done = []
        for o in range(NO):
            par = o % 2
            if len(wqc_marks) >= 2:
                w("sync", t_c, wqc_marks[-2])
            tgt = dma("sync", dwq,
                      WQC[:, 2048 * par:2048 * par + 2048]
                      .rearrange("p (a n) -> p a n", a=16),
                      wqs.rearrange("(a p) n -> p a n", p=128)
                      [:, :, 128 * o:128 * o + 128])
            w("tensor", dwq, tgt)
            if o >= 2:
                w("tensor", v_c, evac_done[o - 2])
            for i in range(NI):
                last = (i == NI - 1)
                mk = op("tensor", t_c if last else None,
                        lambda e, i=i, par=par, last=last: e.matmul(
                            out=PB[5 + par][:, 0:512],
                            lhsT=WQC[:, 2048 * par + 128 * i:
                                      2048 * par + 128 * i + 128],
                            rhs=XN[:, 512 * i:512 * i + 512],
                            start=(i == 0), stop=last, skip_group_check=True))
                if last:
                    wqc_marks.append(mk)
            w("vector", t_c)
            co = 512 * o
            op("vector", v_c, lambda e, co=co, par=par: e.scalar_tensor_tensor(
                out=DSTF[:, co:co + 512], in0=PB[5 + par][:, 0:512],
                scalar=SCL[:, 3:4], in1=SRC[:, co:co + 512],
                op0=ALU.mult, op1=ALU.add))
            ev = op("vector", v_c, lambda e, co=co, o=o: e.tensor_scalar(
                out=DSTF[:, co:co + 512], in0=DSTF[:, co:co + 512],
                scalar1=FBC[:, o:o + 1], scalar2=None, op0=ALU.add))
            evac_done.append(ev)
            if store_to is not None:
                w("gpsimd", v_c, ev)
                dma("gpsimd", dx1, store_to[128 * o:128 * o + 128, tc0:tc0 + 512],
                    DSTF[:, co:co + 512])
        if store_to is not None:
            ffn_state["prev_store"] = dx1.n

    w("sync", dx1, wq_done)
    xblk_free = 0
    for tb in range(NTB):
        tc0 = TB * tb
        if tb >= 1:
            w("sync", v_c, xblk_free)
        for i in range(NI):
            dma("sync", din, XBLK[:, 512 * i:512 * i + 512],
                xT[128 * i:128 * i + 128, tc0:tc0 + TB])
        w("vector", din)
        w("tensor", din)
        emit_ln(XBLK)
        emit_ffn(XBLK, X1B, x1s, tc0)
        xblk_free = v_c.n
        emit_ln(X1B)

        w("tensor", v_c)
        for p in range(7):
            for i in range(NI):
                last = (i == NI - 1)
                op("tensor", t_c if last else None,
                   lambda e, p=p, i=i, last=last: e.matmul(
                       out=PB[p][:, 0:512],
                       lhsT=PRJ[:, 896 * i + 128 * p:896 * i + 128 * p + 128],
                       rhs=XN[:, 512 * i:512 * i + 512],
                       start=(i == 0), stop=last, skip_group_check=True))
        proj_done = t_c.n

        w("vector", t_c, proj_done)
        op("vector", v_c, lambda e: e.tensor_scalar(
            out=PWS[0:3, 0:512], in0=PB[6][0:3, 0:512],
            scalar1=SMALL[0:3, 9:10], scalar2=None, op0=ALU.add))
        w("tensor", v_c)
        for t4 in range(4):
            op("tensor", t_c, lambda e, t4=t4: e.transpose(
                out=AUX[:, 3 * t4:3 * t4 + 3],
                in_=PWS[0:3, 128 * t4:128 * t4 + 128], identity=IDT[0:3, 0:3]))
        w("vector", t_c)
        w("vector", s_c)
        op("vector", v_c, lambda e: e.tensor_reduce(
            out=SCL[:, 4:8], in_=AUX[:, 0:12].rearrange("p (a b) -> p a b", a=4),
            axis=mybir.AxisListType.X, op=ALU.max))
        op("vector", v_c, lambda e, tc0=tc0: e.tensor_copy(
            out=VG[:, tc0:tc0 + 512], in_=PB[1][:, 0:512]))
        op("vector", v_c, lambda e: e.tensor_scalar(
            out=SCL[:, 4:8], in0=SCL[:, 4:8], scalar1=-1.0, scalar2=None,
            op0=ALU.mult))
        op("vector", v_c, lambda e: e.memset(VABS[:, 4:8], 0.0))
        w("scalar", v_c)
        for t4 in range(4):
            op("scalar", s_c, lambda e, t4=t4: e.activation(
                out=XN[:, 1024 + 4 * t4:1024 + 4 * t4 + 3],
                in_=AUX[:, 3 * t4:3 * t4 + 3], func=AF.Exp,
                bias=SCL[:, 4 + t4:5 + t4], scale=1.0))
        w("vector", s_c)
        op("vector", v_c, lambda e: e.tensor_reduce(
            out=SCL[:, 4:8],
            in_=XN[:, 1024:1040].rearrange("p (a b) -> p a b", a=4)[:, :, 0:3],
            axis=mybir.AxisListType.X, op=ALU.add))
        op("vector", v_c, lambda e: e.memset(VABS[:, 4:8], 0.0))
        op("vector", v_c, lambda e: e.memset(VABS[:, 8:12], 0.0))
        op("vector", v_c, lambda e: e.reciprocal(out=SCL[:, 4:8], in_=SCL[:, 4:8]))
        op("vector", v_c, lambda e: e.memset(VABS[:, 8:12], 0.0))
        for t4 in range(4):
            op("vector", v_c, lambda e, t4=t4, tb=tb: e.tensor_scalar(
                out=PWT[:, 12 * tb + 3 * t4:12 * tb + 3 * t4 + 3],
                in0=XN[:, 1024 + 4 * t4:1024 + 4 * t4 + 3],
                scalar1=SCL[:, 4 + t4:5 + t4], scalar2=None, op0=ALU.mult))

        w("tensor", v_c)
        w("tensor", s_c)
        op("tensor", t_c, lambda e, tc0=tc0: e.matmul(
            out=AUX[:, 0:512], lhsT=NMW1[:, 0:128], rhs=VG[:, tc0:tc0 + 512],
            start=True, stop=True, skip_group_check=True))
        op("tensor", t_c, lambda e, tc0=tc0: e.matmul(
            out=PB[6][:, 0:512], lhsT=NMW1[:, 128:256], rhs=VG[:, tc0:tc0 + 512],
            start=True, stop=True, skip_group_check=True))
        w("scalar", t_c)
        op("scalar", s_c, lambda e: e.activation(
            out=XN[:, 0:512], in_=AUX[:, 0:512], func=AF.Gelu,
            bias=NMB1[:, 0:1], scale=1.0))
        op("scalar", s_c, lambda e: e.activation(
            out=XN[:, 512:1024], in_=PB[6][:, 0:512], func=AF.Gelu,
            bias=NMB1[:, 1:2], scale=1.0))
        w("tensor", s_c)
        for h in range(2):
            op("tensor", t_c, lambda e, h=h: e.matmul(
                out=AUX[:, 0:512], lhsT=NMW2[:, 128 * h:128 * h + 128],
                rhs=XN[:, 512 * h:512 * h + 512], start=(h == 0), stop=(h == 1),
                skip_group_check=True))
        w("vector", t_c)
        op("vector", v_c, lambda e, tc0=tc0: e.tensor_scalar(
            out=NUG[:, tc0:tc0 + 512], in0=AUX[:, 0:512],
            scalar1=SMALL[:, 4:5], scalar2=None, op0=ALU.add))

        for (pch, DST) in ((0, KG), (2, QG)):
            op("vector", v_c, lambda e, pch=pch: e.tensor_copy(
                out=XN[:, 0:512], in_=PB[pch][:, 0:512]))
            op("vector", v_c, lambda e: e.scalar_tensor_tensor(
                out=SQT[:, 0:512], in0=XN[:, 0:512], scalar=1.0,
                in1=XN[:, 0:512], op0=ALU.mult, op1=ALU.mult))
            w("tensor", v_c)
            op("tensor", t_c, lambda e: e.matmul(
                out=AUX[0:1, 0:512], lhsT=ONESB[:, 0:1], rhs=SQT[:, 0:512],
                start=True, stop=True, skip_group_check=True))
            w("scalar", t_c)
            op("scalar", s_c, lambda e: e.activation(
                out=ROWS[0:1, 512:1024], in_=AUX[0:1, 0:512], func=AF.Sqrt,
                bias=1e-24, scale=1.0))
            w("vector", s_c)
            op("vector", v_c, lambda e: e.reciprocal(
                out=ROWS[0:1, 0:512], in_=ROWS[0:1, 512:1024]))
            op("vector", v_c, lambda e: e.memset(VABS[:, 4:8], 0.0))
            op("vector", v_c, lambda e: e.tensor_copy(
                out=ROWSB[0:1, 1024:1536], in_=ROWS[0:1, 0:512]))
            w("tensor", v_c)
            op("tensor", t_c, lambda e: e.matmul(
                out=AUX[:, 0:512], lhsT=ONESRB[0:1, :],
                rhs=ROWSB[0:1, 1024:1536], start=True, stop=True,
                skip_group_check=True))
            w("vector", t_c)
            op("vector", v_c, lambda e, tc0=tc0, DST=DST: e.tensor_tensor(
                out=DST[:, tc0:tc0 + 512], in0=XN[:, 0:512], in1=AUX[:, 0:512],
                op=ALU.mult))

        w("scalar", t_c)
        op("scalar", s_c, lambda e: e.activation(
            out=XN[:, 0:512], in_=PB[5][:, 0:512], func=AF.Gelu,
            bias=SMALL[:, 5:6], scale=1.0))
        w("tensor", s_c)
        op("tensor", t_c, lambda e: e.matmul(
            out=AUX[0:1, 0:512], lhsT=SMALLB[:, 0:1], rhs=XN[:, 0:512],
            start=True, stop=True, skip_group_check=True))
        w("scalar", t_c)
        op("scalar", s_c, lambda e: e.activation(
            out=ROWS[0:1, 0:512], in_=AUX[0:1, 0:512], func=AF.Sigmoid,
            bias=SMALL[0:1, 8:9], scale=1.0))
        w("vector", s_c)
        op("vector", v_c, lambda e: e.tensor_scalar(
            out=ROWSB[0:1, 1024:1536], in0=ROWS[0:1, 0:512], scalar1=0.5,
            scalar2=None, op0=ALU.is_gt))
        w("tensor", v_c)
        op("tensor", t_c, lambda e: e.matmul(
            out=AUX[:, 0:512], lhsT=ONESRB[0:1, :], rhs=ROWSB[0:1, 1024:1536],
            start=True, stop=True, skip_group_check=True))
        w("vector", t_c)
        op("vector", v_c, lambda e, tc0=tc0: e.tensor_tensor(
            out=KSUG[:, tc0:tc0 + 512], in0=KG[:, tc0:tc0 + 512],
            in1=AUX[:, 0:512], op=ALU.mult))

        w("scalar", t_c)
        op("scalar", s_c, lambda e: e.activation(
            out=XN[:, 0:512], in_=PB[3][:, 0:512], func=AF.Sigmoid,
            bias=SMALL[:, 2:3], scale=1.0))
        op("scalar", s_c, lambda e: e.activation(
            out=XN[:, 512:1024], in_=PB[4][:, 0:512], func=AF.Sigmoid,
            bias=SMALL[:, 3:4], scale=1.0))
        w("vector", s_c)
        op("vector", v_c, lambda e, tc0=tc0: e.tensor_scalar(
            out=LAFG[:, tc0:tc0 + 512], in0=XN[:, 0:512],
            scalar1=LSF[:, 0:1], scalar2=None, op0=ALU.mult))
        op("vector", v_c, lambda e, tc0=tc0: e.tensor_scalar(
            out=LASG[:, tc0:tc0 + 512], in0=XN[:, 512:1024],
            scalar1=LSF[:, 1:2], scalar2=None, op0=ALU.mult))

    expt_free = 0
    for tt in range(16):
        t0 = 128 * tt
        for (st, LG) in ((0, LAFG), (1, LASG)):
            w("tensor", v_c)
            for sub in range(4):
                op("tensor", t_c, lambda e, sub=sub, t0=t0, LG=LG: e.transpose(
                    out=MISC[0:32, 128 * sub:128 * sub + 128],
                    in_=LG[:, t0 + 32 * sub:t0 + 32 * sub + 32],
                    identity=IDT[:, :]))
            w("vector", t_c)
            for sub in range(4):
                op("vector", v_c, lambda e, sub=sub: e.tensor_copy(
                    out=LT32[0:32, 128 * sub:128 * sub + 128],
                    in_=MISC[0:32, 128 * sub:128 * sub + 128]))
            w("tensor", v_c)
            for sub in range(4):
                for ut in range(2):
                    op("tensor", t_c, lambda e, st=st, sub=sub, ut=ut: e.matmul(
                        out=AUX[:, 256 * st + 128 * ut + 32 * sub:
                                256 * st + 128 * ut + 32 * sub + 32],
                        lhsT=LT32[0:32, 128 * sub:128 * sub + 128],
                        rhs=U32[:, 32 * ut:32 * ut + 32],
                        start=True, stop=True, skip_group_check=True))
        w("scalar", t_c)
        for st in range(2):
            b = 0
            sp = 256 * st
            w("scalar", v_c, expt_free)
            LG = LAFG if st == 0 else LASG
            for fn in (
                lambda e, b=b, sp=sp: e.activation(
                    out=EXPT[:, b:b + 128], in_=AUX[:, sp:sp + 128], func=AF.Exp,
                    bias=float(np.log(0.5)), scale=1.0),
                lambda e, b=b, sp=sp: e.activation(
                    out=EXPT[:, b + 128:b + 256], in_=AUX[:, sp:sp + 128],
                    func=AF.Exp),
                lambda e, b=b, sp=sp: e.activation(
                    out=EXPT[:, b + 384:b + 512], in_=AUX[:, sp + 128:sp + 256],
                    func=AF.Exp, scale=-1.0),
                lambda e, b=b, sp=sp: e.activation(
                    out=EXPT[:, b + 512:b + 640], in_=AUX[:, sp + 128:sp + 256],
                    func=AF.Exp),
                lambda e, t0=t0, LG=LG: e.activation(
                    out=X1B[:, 1024:1152], in_=LG[:, t0:t0 + 128], func=AF.Exp),
            ):
                op("scalar", s_c, fn)
            w("vector", s_c)
            QA3 = 6144 * st
            KSRC = KG if st == 0 else KSUG
            MG = MFG if st == 0 else MSG
            DL = DLF if st == 0 else DLS
            blk0 = 96 * 4 * tt
            op("vector", v_c, lambda e: e.tensor_scalar(
                out=SQT[:, 0:128], in0=X1B[:, 1024:1152],
                scalar1=-1.0, scalar2=1.0, op0=ALU.mult, op1=ALU.add))
            op("vector", v_c, lambda e, b=b, t0=t0, QA3=QA3, blk0=blk0: e.tensor_tensor(
                out=PRJ[:, QA3 + blk0:QA3 + blk0 + 384]
                .rearrange("p (a b) -> p a b", b=96)[:, :, 0:32],
                in0=QG[:, t0:t0 + 128].rearrange("p (a b) -> p a b", b=32),
                in1=EXPT[:, b:b + 128].rearrange("p (a b) -> p a b", b=32),
                op=ALU.mult))
            op("vector", v_c, lambda e, b=b, t0=t0, QA3=QA3, blk0=blk0: e.tensor_tensor(
                out=PRJ[:, QA3 + blk0:QA3 + blk0 + 384]
                .rearrange("p (a b) -> p a b", b=96)[:, :, 32:64],
                in0=KG[:, t0:t0 + 128].rearrange("p (a b) -> p a b", b=32),
                in1=EXPT[:, b + 128:b + 256].rearrange("p (a b) -> p a b", b=32),
                op=ALU.mult))
            op("vector", v_c, lambda e, b=b, t0=t0, QA3=QA3, blk0=blk0: e.tensor_tensor(
                out=PRJ[:, QA3 + blk0:QA3 + blk0 + 384]
                .rearrange("p (a b) -> p a b", b=96)[:, :, 64:96],
                in0=QG[:, t0:t0 + 128].rearrange("p (a b) -> p a b", b=32),
                in1=EXPT[:, b + 512:b + 640].rearrange("p (a b) -> p a b", b=32),
                op=ALU.mult))
            op("vector", v_c, lambda e, t0=t0, KSRC=KSRC: e.scalar_tensor_tensor(
                out=SQT[:, 128:256], in0=KSRC[:, t0:t0 + 128], scalar=1.0,
                in1=SQT[:, 0:128], op0=ALU.mult, op1=ALU.mult))
            op("vector", v_c, lambda e, b=b, t0=t0, MG=MG: e.scalar_tensor_tensor(
                out=MG[:, t0:t0 + 128], in0=SQT[:, 128:256], scalar=1.0,
                in1=EXPT[:, b + 384:b + 512], op0=ALU.mult, op1=ALU.mult))
            op("vector", v_c, lambda e, b=b, tt=tt, DL=DL: e.tensor_copy(
                out=DL[:, 4 * tt:4 * tt + 4],
                in_=EXPT[:, b + 512 + 31:b + 640:32]))
            expt_free = v_c.n
        w("vector", s_c)
        for sub in range(4):
            kk = 4 * tt + sub
            ck0 = 32 * kk
            w("tensor", v_c)
            op("tensor", t_c, lambda e, kk=kk, ck0=ck0: e.matmul(
                out=P5[0:64, 0:32], lhsT=PRJ[:, 96 * kk:96 * kk + 64],
                rhs=MFG[:, ck0:ck0 + 32], start=True, stop=True,
                skip_group_check=True))
            op("tensor", t_c, lambda e, kk=kk, ck0=ck0: e.matmul(
                out=P5[0:64, 32:64],
                lhsT=PRJ[:, 6144 + 96 * kk:6144 + 96 * kk + 64],
                rhs=MSG[:, ck0:ck0 + 32], start=True, stop=True,
                skip_group_check=True))
            op("tensor", t_c, lambda e, kk=kk, ck0=ck0: e.matmul(
                out=P5[0:32, 64:96], lhsT=MFG[:, ck0:ck0 + 32],
                rhs=PRJ[:, 96 * kk + 64:96 * kk + 96], start=True, stop=True,
                skip_group_check=True))
            op("tensor", t_c, lambda e, kk=kk, ck0=ck0: e.matmul(
                out=P5[0:32, 96:128], lhsT=MSG[:, ck0:ck0 + 32],
                rhs=PRJ[:, 6144 + 96 * kk + 64:6144 + 96 * kk + 96],
                start=True, stop=True, skip_group_check=True))
            for sl, (QA3, MG2) in enumerate(((0, MFG), (6144, MSG))):
                op("vector", v_c, lambda e, kk=kk, ck0=ck0, QA3=QA3, MG2=MG2, sl=sl:
                   e.scalar_tensor_tensor(
                       out=SQT[:, 64 * sl:64 * sl + 31],
                       in0=PRJ[:, QA3 + 96 * kk + 1:QA3 + 96 * kk + 32],
                       scalar=1.0, in1=MG2[:, ck0:ck0 + 31],
                       op0=ALU.mult, op1=ALU.mult))
                op("vector", v_c, lambda e, kk=kk, ck0=ck0, QA3=QA3, MG2=MG2, sl=sl:
                   e.scalar_tensor_tensor(
                       out=SQT[:, 64 * sl + 32:64 * sl + 63],
                       in0=PRJ[:, QA3 + 96 * kk + 33:QA3 + 96 * kk + 64],
                       scalar=1.0, in1=MG2[:, ck0:ck0 + 31],
                       op0=ALU.mult, op1=ALU.mult))
            w("tensor", v_c)
            for sq in range(4):
                op("tensor", t_c, lambda e, sq=sq: e.matmul(
                    out=P6[0:1, 32 * sq:32 * sq + 31], lhsT=ONESB[:, 0:1],
                    rhs=SQT[:, 32 * sq:32 * sq + 31], start=True, stop=True,
                    skip_group_check=True))
            w("vector", t_c)
            op("vector", v_c, lambda e, ck0=ck0: e.tensor_tensor(
                out=LAFG[0:64, ck0:ck0 + 32], in0=P5[0:64, 0:32],
                in1=MASKAK[:, :], op=ALU.mult))
            op("vector", v_c, lambda e, ck0=ck0: e.tensor_tensor(
                out=LASG[0:64, ck0:ck0 + 32], in0=P5[0:64, 32:64],
                in1=MASKAK[:, :], op=ALU.mult))
            op("vector", v_c, lambda e, ck0=ck0: e.tensor_tensor(
                out=LAFG[64:96, ck0:ck0 + 32], in0=P5[0:32, 64:96],
                in1=MASKB[:, :], op=ALU.mult))
            op("vector", v_c, lambda e, ck0=ck0: e.tensor_tensor(
                out=LASG[64:96, ck0:ck0 + 32], in0=P5[0:32, 96:128],
                in1=MASKB[:, :], op=ALU.mult))
            for slot, sq in ((0, 0), (1, 2), (2, 1), (3, 3)):
                op("vector", v_c, lambda e, slot=slot, sq=sq: e.tensor_scalar(
                    out=ROWS[0:1, 4 + slot:128:4], in0=P6[0:1, 32 * sq:32 * sq + 31],
                    scalar1=-1.0, scalar2=None, op0=ALU.mult))
            w("tensor", v_c)
            op("tensor", t_c, lambda e: e.transpose(
                out=P6[0:128, 128:129], in_=ROWS[0:1, 0:128],
                identity=IDT[0:1, 0:1]))
            w("vector", t_c)
            op("vector", v_c, lambda e, kk=kk: e.tensor_copy(
                out=GDALL2[:, kk:kk + 1], in_=P6[:, 128:129]))

    prep_done = v_c.n
    w("tensor", v_c)
    op("tensor", t_c, lambda e: e.transpose(
        out=MISC[0:1, 0:128], in_=GDALL2[:, 0:1], identity=IDT[:, :]))
    w("vector", t_c)
    op("vector", v_c, lambda e: e.tensor_copy(out=GDIAG[0:1, 0:128],
                                              in_=MISC[0:1, 0:128]))
    w("gpsimd", v_c, prep_done)
    for k in range(SC):
        par = k % 2
        gb = 0
        gdb = par * 4 * c
        t0 = c * k
        pp = par * 256

        op("vector", v_c, lambda e, t0=t0: e.tensor_copy(
            out=GBJF[:, :], in_=LAFG[64:96, t0:t0 + 32]))
        op("vector", v_c, lambda e, t0=t0: e.tensor_copy(
            out=GBJS[:, :], in_=LASG[64:96, t0:t0 + 32]))
        if k + 1 < SC:
            w("tensor", v_c)
            op("tensor", t_c, lambda e, k=k: e.transpose(
                out=MISC[0:1, 0:128], in_=GDALL2[:, k + 1:k + 2],
                identity=IDT[:, :]))
            w("vector", t_c)
            op("vector", v_c, lambda e, k=k: e.tensor_copy(
                out=GDIAG[0:1, 128 * ((k + 1) % 2):128 * ((k + 1) % 2) + 128],
                in_=MISC[0:1, 0:128]))
        w("gpsimd", v_g, 32 * k)
        for (po, blk) in ((0, 0), (32, 4)):
            dma("gpsimd", dga,
                GALLE[0:1, gb:gb + c * B7]
                .rearrange("p (a b) -> p a b", b=B7)[:, :, blk * c:blk * c + c],
                LAFG[po:po + 32, t0:t0 + 32])
        for (po, blk) in ((0, 2), (32, 5)):
            dma("gpsimd", dga,
                GALLE[0:1, gb:gb + c * B7]
                .rearrange("p (a b) -> p a b", b=B7)[:, :, blk * c:blk * c + c],
                LASG[po:po + 32, t0:t0 + 32])
        ga_done = dga.n

        w("tensor", v_c)
        op("tensor", t_c, lambda e, k=k, pp=pp: e.matmul(
            out=PSF[:, pp:pp + 96], lhsT=F0[:, :],
            rhs=PRJ[:, 96 * k:96 * k + 96], start=True, stop=True,
            skip_group_check=True))
        base_t = op("tensor", t_c, lambda e, k=k, pp=pp: e.matmul(
            out=PSS[:, pp:pp + 96], lhsT=S0[:, :],
            rhs=PRJ[:, 6144 + 96 * k:6144 + 96 * k + 96], start=True, stop=True,
            skip_group_check=True))
        w("vector", t_c, base_t)
        op("vector", v_c, lambda e, pp=pp: e.tensor_copy(
            out=PRE32[:, 0:32], in_=PSS[:, pp:pp + 32]))
        op("vector", v_c, lambda e, pp=pp: e.scalar_tensor_tensor(
            out=PRE32[:, 0:32], in0=PSF[:, pp:pp + 32], scalar=1.0,
            in1=PRE32[:, 0:32], op0=ALU.mult, op1=ALU.add))
        op("vector", v_c, lambda e, t0=t0: e.tensor_tensor(
            out=WORK[:, 2 * c:3 * c], in0=PRE32[:, 0:32],
            in1=VG[:, t0:t0 + 32], op=ALU.subtract))
        op("vector", v_c, lambda e, pp=pp, t0=t0: e.tensor_tensor(
            out=WORK[:, 0:c], in0=PSF[:, pp + 32:pp + 64],
            in1=VG[:, t0:t0 + 32], op=ALU.subtract))
        op("vector", v_c, lambda e, pp=pp, t0=t0: e.tensor_tensor(
            out=WORK[:, 4 * c:5 * c], in0=PSS[:, pp + 32:pp + 64],
            in1=VG[:, t0:t0 + 32], op=ALU.subtract))
        op("vector", v_c, lambda e: e.memset(SVEC[:, :], 1.0))
        w("vector", dga, ga_done)
        v0 = v_g.n
        for i in (0, 1):
            op("vector", v_g, lambda e, i=i, gb=gb: e.scalar_tensor_tensor(
                out=GROW[0:1, B7 * (i % 2):B7 * (i % 2) + B7]
                .rearrange("p (a b) -> p a b", a=7),
                in0=GALLE[0:1, gb + B7 * i:gb + B7 * i + B7]
                .rearrange("p (a b) -> p a b", a=7),
                scalar=1.0,
                in1=SVEC[0:1, :].unsqueeze(1).to_broadcast([1, 7, c]),
                op0=ALU.mult, op1=ALU.mult))

        for i in range(c):
            gstep = c * k + i
            bc = BCB[i % 2]
            w("tensor", v_g, v0 + i + 1)
            if gstep >= 2:
                w("tensor", v_w, 2 * (gstep - 1))
            if i > 0:
                op("tensor", None, lambda e, bc=bc, pi=i % 2: e.matmul(
                    out=bc[:, 0:B7], lhsT=ONESRB[0:1, :],
                    rhs=GROW[0:1, B7 * pi:B7 * pi + B7],
                    start=True, stop=False, skip_group_check=True))
                w("tensor", s_d, s_d.n)
                sbb = SVEC[0:1, i - 1:i].to_broadcast([1, 128])
                op("tensor", None, lambda e, bc=bc, sbb=sbb, gdb=gdb, i=i: e.matmul(
                    out=bc[:, i - 1:2 * c + i:2 * c], lhsT=sbb,
                    rhs=GDIAG[0:1, gdb + 4 * i:gdb + 4 * i + 2],
                    start=False, stop=False, skip_group_check=True))
                op("tensor", t_bc, lambda e, bc=bc, sbb=sbb, gdb=gdb, i=i: e.matmul(
                    out=bc[:, 4 * c + i - 1:5 * c + i:c], lhsT=sbb,
                    rhs=GDIAG[0:1, gdb + 4 * i + 2:gdb + 4 * i + 4],
                    start=False, stop=True, skip_group_check=True))
            else:
                op("tensor", t_bc, lambda e, bc=bc, pi=i % 2: e.matmul(
                    out=bc[:, 0:B7], lhsT=ONESRB[0:1, :],
                    rhs=GROW[0:1, B7 * pi:B7 * pi + B7],
                    start=True, stop=True, skip_group_check=True))
            w("vector", t_bc, t_bc.n)
            op("vector", None, lambda e, bc=bc: e.scalar_tensor_tensor(
                out=JUNKV[:, 0:3 * c], in0=WORK[:, c:4 * c], scalar=1.0,
                in1=bc[:, 0:3 * c], op0=ALU.mult, op1=ALU.mult,
                accum_out=DCOL[:, 0:1]))
            op("vector", v_w, lambda e, bc=bc, i=i: e.scalar_tensor_tensor(
                out=JUNKV[:, 0:2 * c], in0=WORK[:, 0:2 * c], scalar=1.0,
                in1=bc[:, 3 * c:5 * c], op0=ALU.mult, op1=ALU.mult,
                accum_out=WORK[:, c + i:c + i + 1]))
            op("vector", v_w, lambda e, bc=bc, i=i: e.scalar_tensor_tensor(
                out=JUNKV[:, 0:2 * c], in0=WORK[:, 3 * c:5 * c], scalar=1.0,
                in1=bc[:, 5 * c:7 * c], op0=ALU.mult, op1=ALU.mult,
                accum_out=WORK[:, 3 * c + i:3 * c + i + 1]))
            if i + 2 < c:
                w("vector", s_d, s_d.n + 1)
                op("vector", v_g, lambda e, i=i, gb=gb: e.scalar_tensor_tensor(
                    out=GROW[0:1, B7 * (i % 2):B7 * (i % 2) + B7]
                    .rearrange("p (a b) -> p a b", a=7),
                    in0=GALLE[0:1, gb + B7 * (i + 2):gb + B7 * (i + 2) + B7]
                    .rearrange("p (a b) -> p a b", a=7),
                    scalar=1.0,
                    in1=SVEC[0:1, :].unsqueeze(1).to_broadcast([1, 7, c]),
                    op0=ALU.mult, op1=ALU.mult))
            w("tensor", v_w, v_w.n)
            op("tensor", t_e, lambda e, gstep=gstep: e.matmul(
                out=MISC[0:1, 384 + gstep % 8:384 + gstep % 8 + 1],
                lhsT=DCOL[:, 0:1], rhs=DCOL[:, 0:1], start=True, stop=True,
                skip_group_check=True))
            w("scalar", t_e, t_e.n)
            op("scalar", s_d, lambda e, gstep=gstep, i=i: e.activation(
                out=SVEC[0:1, i:i + 1],
                in_=MISC[0:1, 384 + gstep % 8:384 + gstep % 8 + 1],
                func=AF.Sigmoid, scale=float(1.0 / (1.0 + 1e-6))))

        w("vector", s_d, s_d.n)
        op("vector", v_c, lambda e, t0=t0: e.tensor_copy(
            out=SROW[0:1, t0:t0 + c], in_=SVEC[0:1, :]))
        op("vector", v_c, lambda e: e.tensor_copy(out=SVF32[0:1, :],
                                                  in_=SVEC[0:1, :]))
        op("vector", v_c, lambda e, t0=t0: e.tensor_copy(
            out=MCP[:, 0:32], in_=MFG[:, t0:t0 + 32]))
        vpre = op("vector", v_c, lambda e, t0=t0: e.tensor_copy(
            out=MCP[:, 32:64], in_=MSG[:, t0:t0 + 32]))
        w("tensor", v_c, vpre)
        op("tensor", t_c, lambda e: e.transpose(
            out=MISC[0:32, 480:481], in_=SVF32[0:1, :], identity=IDT[0:1, 0:1]))
        op("tensor", t_c, lambda e: e.transpose(
            out=MISC[0:32, 192:320], in_=WORK[:, c:2 * c], identity=IDT[:, :]))
        op("tensor", t_c, lambda e: e.transpose(
            out=MISC[0:32, 320:448], in_=WORK[:, 3 * c:4 * c], identity=IDT[:, :]))
        op("tensor", t_c, lambda e: e.transpose(
            out=AUX[0:32, 0:128], in_=MCP[:, 0:32], identity=IDT[:, :]))
        tr_done = op("tensor", t_c, lambda e: e.transpose(
            out=AUX[0:32, 128:256], in_=MCP[:, 32:64], identity=IDT[:, :]))
        w("vector", t_c, tr_done)
        op("vector", v_c, lambda e: e.tensor_scalar(
            out=WTRF[:, :], in0=MISC[0:32, 192:320], scalar1=MISC[0:32, 480:481],
            scalar2=-1.0, op0=ALU.mult, op1=ALU.mult))
        op("vector", v_c, lambda e: e.tensor_scalar(
            out=WTRS[:, :], in0=MISC[0:32, 320:448], scalar1=MISC[0:32, 480:481],
            scalar2=-1.0, op0=ALU.mult, op1=ALU.mult))
        op("vector", v_c, lambda e: e.tensor_copy(out=MTF[:, :], in_=AUX[0:32, 0:128]))
        vmt = op("vector", v_c, lambda e: e.tensor_copy(out=MTS[:, :],
                                                        in_=AUX[0:32, 128:256]))
        w("tensor", v_c, vmt)
        op("tensor", t_c, lambda e, pp=pp: e.matmul(
            out=PSF[:, pp + 64:pp + 96], lhsT=WTRF[:, :], rhs=GBJF[:, :],
            start=False, stop=True, skip_group_check=True))
        op("tensor", t_c, lambda e, pp=pp: e.matmul(
            out=PSS[:, pp + 64:pp + 96], lhsT=WTRS[:, :], rhs=GBJS[:, :],
            start=False, stop=True, skip_group_check=True))
        op("tensor", t_c, lambda e: e.matmul(
            out=P5[:, 0:128], lhsT=MTF[:, :], rhs=WTRF[:, :],
            start=True, stop=True, skip_group_check=True))
        reb_t = op("tensor", t_c, lambda e: e.matmul(
            out=P6[:, 0:128], lhsT=MTS[:, :], rhs=WTRS[:, :],
            start=True, stop=True, skip_group_check=True))
        w("vector", t_c, reb_t)
        op("vector", v_c, lambda e, pp=pp, t0=t0: e.tensor_copy(
            out=OUTF[:, t0:t0 + 32], in_=PSF[:, pp + 64:pp + 96]))
        op("vector", v_c, lambda e, pp=pp, t0=t0: e.tensor_copy(
            out=OUTS[:, t0:t0 + 32], in_=PSS[:, pp + 64:pp + 96]))
        op("vector", v_c, lambda e: e.scalar_tensor_tensor(
            out=REBT[:, :], in0=P5[:, 0:128], scalar=1.0, in1=F0[:, :],
            op0=ALU.mult, op1=ALU.add))
        op("vector", v_c, lambda e, k=k: e.tensor_scalar(
            out=F0[:, :], in0=REBT[:, :], scalar1=DLF[:, k:k + 1], scalar2=None,
            op0=ALU.mult))
        op("vector", v_c, lambda e: e.scalar_tensor_tensor(
            out=REBT[:, :], in0=P6[:, 0:128], scalar=1.0, in1=S0[:, :],
            op0=ALU.mult, op1=ALU.add))
        op("vector", v_c, lambda e, k=k: e.tensor_scalar(
            out=S0[:, :], in0=REBT[:, :], scalar1=DLS[:, k:k + 1], scalar2=None,
            op0=ALU.mult))

    op("vector", v_c, lambda e: e.tensor_scalar(
        out=ROWSB[0:1, 0:2048], in0=SROW[0:1, :], scalar1=0.7, scalar2=None,
        op0=ALU.is_gt))
    op("vector", v_c, lambda e: e.tensor_scalar(
        out=SROW[0:1, :], in0=ROWSB[0:1, 0:2048], scalar1=-0.1,
        scalar2=1.0, op0=ALU.mult, op1=ALU.add))
    op("vector", v_c, lambda e: e.tensor_scalar(
        out=ROWSB[0:1, 0:2048], in0=ROWSB[0:1, 0:2048], scalar1=0.1,
        scalar2=None, op0=ALU.mult))
    for q in range(4):
        w("tensor", v_c)
        op("tensor", t_c, lambda e, q=q: e.matmul(
            out=MISC[:, 0:512], lhsT=ONESRB[0:1, :],
            rhs=SROW[0:1, 512 * q:512 * q + 512], start=True,
            stop=True, skip_group_check=True))
        op("tensor", t_c, lambda e, q=q: e.matmul(
            out=AUX[:, 0:512], lhsT=ONESRB[0:1, :],
            rhs=ROWSB[0:1, 512 * q:512 * q + 512], start=True, stop=True,
            skip_group_check=True))
        w("vector", t_c)
        op("vector", v_c, lambda e, q=q: e.tensor_copy(
            out=XN[:, 512 * q:512 * q + 512], in_=MISC[:, 0:512]))
        op("vector", v_c, lambda e, q=q: e.tensor_tensor(
            out=XN[:, 2048 + 512 * q:2048 + 512 * q + 512],
            in0=NUG[:, 512 * q:512 * q + 512], in1=AUX[:, 0:512], op=ALU.mult))
    op("vector", v_c, lambda e: e.tensor_tensor_scan(
        out=NSG[:, :], data0=XN[:, 0:2048], data1=XN[:, 2048:4096],
        initial=0.0, op0=ALU.mult, op1=ALU.add))
    w("sync", v_c)
    won_ld = dma("sync", din, NUG[:, :], wos[:, :])

    for tb in range(NTB):
        tc0 = TB * tb
        for i in range(NI):
            dma("gpsimd", dx1, X1B[:, 512 * i:512 * i + 512],
                x1s[128 * i:128 * i + 128, tc0:tc0 + 512])
        x1_ld = dx1.n
        for t4 in range(4):
            tt = 4 * tb + t4
            t0 = 128 * tt
            w("tensor", v_c)
            op("tensor", t_c, lambda e, t0=t0: e.transpose(
                out=MISC[:, 0:128], in_=OUTF[:, t0:t0 + 128], identity=IDT[:, :]))
            op("tensor", t_c, lambda e, t0=t0: e.transpose(
                out=MISC[:, 128:256], in_=OUTS[:, t0:t0 + 128], identity=IDT[:, :]))
            w("vector", t_c, t_c.n - 2)
            op("vector", v_c, lambda e, t0=t0: e.tensor_copy(
                out=OTC[:, :], in_=NSG[:, t0:t0 + 128]))
            w("tensor", v_c)
            op("tensor", t_c, lambda e: e.transpose(
                out=MISC[:, 256:384], in_=OTC[:, :], identity=IDT[:, :]))
            w("vector", t_c)
            op("vector", v_c, lambda e, tt=tt: e.tensor_scalar(
                out=XBLK[:, 4096:4224], in0=MISC[:, 0:128],
                scalar1=PWT[:, 3 * tt:3 * tt + 1], scalar2=None, op0=ALU.mult))
            op("vector", v_c, lambda e, tt=tt: e.tensor_scalar(
                out=XBLK[:, 4224:4352], in0=MISC[:, 128:256],
                scalar1=PWT[:, 3 * tt + 1:3 * tt + 2], scalar2=None, op0=ALU.mult))
            op("vector", v_c, lambda e: e.tensor_tensor(
                out=XBLK[:, 4096:4224], in0=XBLK[:, 4096:4224],
                in1=XBLK[:, 4224:4352], op=ALU.add))
            op("vector", v_c, lambda e, tt=tt: e.tensor_scalar(
                out=XBLK[:, 4224:4352], in0=MISC[:, 256:384],
                scalar1=PWT[:, 3 * tt + 2:3 * tt + 3], scalar2=None, op0=ALU.mult))
            op("vector", v_c, lambda e: e.tensor_tensor(
                out=XBLK[:, 4096:4224], in0=XBLK[:, 4096:4224],
                in1=XBLK[:, 4224:4352], op=ALU.add))
            op("vector", v_c, lambda e: e.tensor_tensor(
                out=XBLK[:, 4224:4352], in0=XBLK[:, 4096:4224],
                in1=XBLK[:, 4096:4224], op=ALU.mult))
            op("vector", v_c, lambda e: e.tensor_reduce(
                out=SCL[:, 4:5], in_=XBLK[:, 4224:4352],
                axis=mybir.AxisListType.X, op=ALU.add))
            op("vector", v_c, lambda e: e.memset(VABS[:, 4:8], 0.0))
            w("scalar", v_c)
            op("scalar", s_c, lambda e: e.activation(
                out=SCL[:, 5:6], in_=SCL[:, 4:5], func=AF.Sqrt,
                bias=RMS_EPS, scale=1.0 / 128.0))
            w("vector", s_c)
            op("vector", v_c, lambda e: e.reciprocal(
                out=SCL[:, 6:7], in_=SCL[:, 5:6]))
            op("vector", v_c, lambda e: e.memset(VABS[:, 4:8], 0.0))
            op("vector", v_c, lambda e: e.tensor_scalar(
                out=OTC[:, :], in0=XBLK[:, 4096:4224], scalar1=SCL[:, 6:7],
                scalar2=None, op0=ALU.mult))
            w("tensor", v_c)
            op("tensor", t_c, lambda e: e.transpose(
                out=MISC[:, 0:128], in_=OTC[:, :], identity=IDT[:, :]))
            w("vector", t_c)
            op("vector", v_c, lambda e, t4=t4: e.tensor_copy(
                out=KG[:, 128 * t4:128 * t4 + 128], in_=MISC[:, 0:128]))
        otv_done = v_c.n
        w("vector", dx1, x1_ld)
        w("tensor", din, won_ld)
        for o in range(NO):
            par = o % 2
            w("tensor", v_c)
            op("tensor", t_c, lambda e, o=o, par=par: e.matmul(
                out=PB[5 + par][:, 0:512], lhsT=NUG[:, 128 * o:128 * o + 128],
                rhs=KG[:, 0:512], start=True, stop=True, skip_group_check=True))
            w("vector", t_c)
            op("vector", v_c, lambda e, o=o, par=par: e.tensor_tensor(
                out=XBLK[:, 512 * o:512 * o + 512], in0=PB[5 + par][:, 0:512],
                in1=X1B[:, 512 * o:512 * o + 512], op=ALU.add))
        emit_ln(XBLK)
        emit_ffn(XBLK, X1B, yT, tc0)

    w("sync", v_c)
    dma("sync", dst, dbg_out, DBGT[:, :])
    w("sync", dx1, dx1.n)
    w("sync", dst, dst.n)

    with nc.Block() as block:
        @block.sync
        def _(e):
            for f in ops["sync"]:
                f(e)

        @block.vector
        def _(e):
            for f in ops["vector"]:
                f(e)

        @block.scalar
        def _(e):
            for f in ops["scalar"]:
                f(e)

        @block.tensor
        def _(e):
            for f in ops["tensor"]:
                f(e)

        @block.gpsimd
        def _(e):
            for f in ops["gpsimd"]:
                f(e)

    for m in reversed(stack):
        m.__exit__(None, None, None)
    return nc


_NC_CACHE = {}


def _consts():
    ident = np.eye(128, dtype=np.float32)
    us = np.zeros((32, 32), np.float32)
    ui = np.zeros((32, 32), np.float32)
    for t in range(32):
        for i in range(32):
            us[t, i] = 1.0 if t < i else 0.0
            ui[t, i] = 1.0 if t <= i else 0.0
    u = np.concatenate([us, ui], axis=1).astype(np.float32)
    maskak = np.zeros((64, 32), BF)
    for i in range(32):
        for j in range(32):
            v = -1.0 if j <= i - 2 else 0.0
            maskak[i, j] = BF(v)
            maskak[32 + i, j] = BF(v)
    maskb = np.zeros((32, 32), BF)
    for j in range(32):
        for i in range(32):
            maskb[j, i] = BF(1.0 if j <= i else 0.0)
    galle = np.zeros((1, c * B7), BF)
    for i in range(c):
        for blk in (1, 3, 6):
            galle[0, B7 * i + blk * c + i] = BF(1.0)
    return ident, u, maskak, maskb, galle


def kernel(L0_attn_x, L0_attn_Wk, L0_attn_Wv, L0_attn_Wq, L0_attn_fast_lambda,
           L0_attn_slow_lambda, L0_attn_Waf, L0_attn_baf, L0_attn_Was,
           L0_attn_bas, L0_attn_nm_w1, L0_attn_nm_b1, L0_attn_nm_w2,
           L0_attn_nm_b2, L0_attn_ir_w1, L0_attn_ir_b1, L0_attn_ir_w2,
           L0_attn_ir_b2, L0_attn_pw_w, L0_attn_pw_b, L0_attn_Wo,
           L0_attn_norm_w, L0_ffn_weight, L0_ffn_bias):
    x = np.ascontiguousarray(np.asarray(L0_attn_x, np.float32))
    B = x.shape[0]
    ident, u, maskak, maskb, galle = _consts()

    projcat = np.zeros((D, 896), np.float32)
    projcat[:, 0:128] = np.asarray(L0_attn_Wk, np.float32)
    projcat[:, 128:256] = np.asarray(L0_attn_Wv, np.float32)
    projcat[:, 256:384] = np.asarray(L0_attn_Wq, np.float32)
    projcat[:, 384:512] = np.asarray(L0_attn_Waf, np.float32)
    projcat[:, 512:640] = np.asarray(L0_attn_Was, np.float32)
    projcat[:, 640:768] = np.asarray(L0_attn_ir_w1, np.float32)
    projcat[:, 768:771] = np.asarray(L0_attn_pw_w, np.float32)

    smallv = np.zeros((128, 16), np.float32)
    smallv[:, 0] = np.asarray(L0_attn_fast_lambda, np.float32)
    smallv[:, 1] = np.asarray(L0_attn_slow_lambda, np.float32)
    smallv[:, 2] = np.asarray(L0_attn_baf, np.float32)
    smallv[:, 3] = np.asarray(L0_attn_bas, np.float32)
    smallv[:, 4] = np.asarray(L0_attn_nm_b2, np.float32)
    smallv[:, 5] = np.asarray(L0_attn_ir_b1, np.float32)
    smallv[:, 6] = np.asarray(L0_attn_norm_w, np.float32)
    smallv[:, 7] = np.asarray(L0_attn_ir_w2, np.float32)[:, 0]
    smallv[0, 8] = np.float32(np.asarray(L0_attn_ir_b2, np.float32).reshape(-1)[0])
    smallv[0:3, 9] = np.asarray(L0_attn_pw_b, np.float32)

    nmb1 = np.zeros((128, 2), np.float32)
    nmb1[:, 0] = np.asarray(L0_attn_nm_b1, np.float32)[0:128]
    nmb1[:, 1] = np.asarray(L0_attn_nm_b1, np.float32)[128:256]
    fb = 0.5 * np.asarray(L0_ffn_bias, np.float32).reshape(16, 128).T.copy()
    wTh = np.ascontiguousarray(np.asarray(L0_ffn_weight, np.float32).T)

    if "nc" not in _NC_CACHE:
        _NC_CACHE["nc"] = build_program()
    nc = _NC_CACHE["nc"]

    in_maps = []
    for core in range(8):
        b = core % B
        in_maps.append({
            "xT": np.ascontiguousarray(x[b].T),
            "wT": wTh,
            "projcat": projcat,
            "nmw1_in": np.asarray(L0_attn_nm_w1, np.float32),
            "nmw2_in": np.asarray(L0_attn_nm_w2, np.float32),
            "wo_in": np.asarray(L0_attn_Wo, np.float32),
            "smallv": smallv,
            "nmb1_in": nmb1,
            "fbias": fb,
            "cident": ident,
            "cU": u,
            "maskak_in": maskak,
            "maskb_in": maskb,
            "galle_in": galle,
            "cbias_in": np.tile(np.array([[1e-8, np.log(0.5), 1e-5, 1e-24,
                                           1e-6, 1e-10, 0.0, 0.0]], np.float32),
                                (128, 1)),
        })

    res = run_bass_kernel_spmd(nc, in_maps, core_ids=list(range(8)))
    _NC_CACHE["res"] = res
    out = np.empty((B, T, D), np.float32)
    for b in range(B):
        out[b] = np.ascontiguousarray(res.results[b]["yT"].T)
    return out
